# version 04
# baseline (speedup 1.0000x reference)
import numpy as np
from contextlib import ExitStack

import concourse.bass as bass
import concourse.mybir as mybir
from concourse.bass import IndirectOffsetOnAxis
from concourse.bass_utils import run_bass_kernel_spmd

F32 = mybir.dt.float32
BF16 = mybir.dt.bfloat16
I32 = mybir.dt.int32
AF = mybir.ActivationFunctionType
ALU = mybir.AluOpType
AX = mybir.AxisListType

ENGS = ("tensor", "vector", "scalar", "gpsimd", "sync")
NCORES = 8
S = 2048
D = 1024
NT = S // 128
ND = D // 128


class Prog:
    def __init__(self, name, num_devices=NCORES):
        self.nc = bass.Bass("TRN2", target_bir_lowering=False,
                            num_devices=num_devices)
        self.es = ExitStack()
        self.es.__enter__()
        self.chain = {e: self.es.enter_context(self.nc.semaphore(f"ch_{e}"))
                      for e in ENGS}
        self.cnt = {e: 0 for e in ENGS}
        self.ops = {e: [] for e in ENGS}
        self.dsem = {}
        self.dcnt = {}
        self.buf = {}
        self.waited = {e: {} for e in ENGS}

    def dram(self, name, shape, dtype, kind="Internal", **kw):
        return self.nc.dram_tensor(name, shape, dtype, kind=kind, **kw)

    def sbuf(self, name, shape, dtype):
        return self.es.enter_context(self.nc.sbuf_tensor(name, shape, dtype))

    def psum(self, name, shape, dtype=F32):
        return self.es.enter_context(self.nc.psum_tensor(name, shape, dtype))

    def _get(self, key):
        return self.buf.setdefault(key, {"w": None, "r": []})

    def _evs_for(self, reads, writes):
        evs = []
        for k in reads:
            b = self._get(k)
            if b["w"] is not None:
                evs.append(b["w"])
        for k in writes:
            b = self._get(k)
            if b["w"] is not None:
                evs.append(b["w"])
            evs.extend(b["r"])
        return evs

    def _sem_of(self, ev):
        if ev[0] == "c":
            return self.chain[ev[1]], ("c", ev[1])
        return self.dsem[ev[1]], ("d", ev[1])

    def _waits(self, evs, eng):
        need = {}
        for ev in evs:
            sem, sid = self._sem_of(ev)
            v = ev[2]
            if ev[0] == "d":
                v = self.dcnt[ev[1]]
            if self.waited[eng].get(sid, -1) >= v:
                continue
            if sid not in need or need[sid][1] < v:
                need[sid] = (sem, v)
        out = []
        for sid, (sem, v) in need.items():
            self.waited[eng][sid] = v
            out.append((sem, v))
        return out

    def _note(self, reads, writes, ev):
        for k in reads:
            self._get(k)["r"].append(ev)
        for k in writes:
            self.buf[k] = {"w": ev, "r": []}

    def op(self, eng, fn, reads=(), writes=()):
        waits = self._waits(self._evs_for(reads, writes), eng)
        self.cnt[eng] += 1
        ev = ("c", eng, self.cnt[eng])
        self.ops[eng].append((waits, fn, [(self.chain[eng], 1)]))
        self._note(reads, writes, ev)
        return ev

    def ops_batch(self, eng, fns, reads=(), writes=()):
        waits = self._waits(self._evs_for(reads, writes), eng)
        self.cnt[eng] += 1
        ev = ("c", eng, self.cnt[eng])
        for i, fn in enumerate(fns):
            incs = [(self.chain[eng], 1)] if i == len(fns) - 1 else []
            self.ops[eng].append((waits if i == 0 else [], fn, incs))
        self._note(reads, writes, ev)
        return ev

    def _dma_ev(self, sem):
        if sem not in self.dsem:
            self.dsem[sem] = self.es.enter_context(self.nc.semaphore(sem))
            self.dcnt[sem] = 0
        self.dcnt[sem] += 16
        return ("d", sem, self.dcnt[sem])

    def dma(self, qeng, out_ap, in_ap, reads=(), writes=(), sem=None):
        sem = sem or ("dq_" + qeng)
        waits = self._waits(self._evs_for(reads, writes), qeng)
        ev = self._dma_ev(sem)
        h = self.dsem[sem]

        def fn(e, o=out_ap, i=in_ap):
            return e.dma_start(out=o, in_=i)
        self.ops[qeng].append((waits, fn, [(h, 16)]))
        self._note(reads, writes, ev)
        return ev

    def idma(self, out_ap, out_off, in_ap, in_off, reads=(), writes=(),
             sem="dq_ix", compute_op=None):
        waits = self._waits(self._evs_for(reads, writes), "gpsimd")
        ev = self._dma_ev(sem)
        h = self.dsem[sem]
        kw = {}
        if compute_op is not None:
            kw["compute_op"] = compute_op

        def fn(e, o=out_ap, oo=out_off, i=in_ap, io=in_off, kw=kw):
            return e.indirect_dma_start(out=o, out_offset=oo, in_=i,
                                        in_offset=io, **kw)
        self.ops["gpsimd"].append((waits, fn, [(h, 16)]))
        self._note(reads, writes, ev)
        return ev

    def matmuls(self, psum_key, items, start=True, stop=True, each=False,
                extra_reads=(), tp=None):
        reads = set(extra_reads)
        for _, _, _, rk in items:
            reads.update(rk)
        n = len(items)
        fns = []
        for j, (lt, rh, oa, _) in enumerate(items):
            st = True if each else (start and j == 0)
            sp = True if each else (stop and j == n - 1)

            def fn(e, lt=lt, rh=rh, oa=oa, st=st, sp=sp, tp=tp):
                if tp:
                    return e.matmul(oa, lt, rh, start=st, stop=sp,
                                    tile_position=(0, tp))
                return e.matmul(oa, lt, rh, start=st, stop=sp)
            fns.append(fn)
        return self.ops_batch("tensor", fns, reads=sorted(reads),
                              writes=(psum_key,))

    def transpose(self, out_ap, psum_key, in_ap, in_keys, ident, ident_key):
        def fn(e, o=out_ap, i=in_ap, d=ident):
            return e.transpose(o, i, d)
        return self.ops_batch("tensor", [fn],
                              reads=tuple(in_keys) + (ident_key,),
                              writes=(psum_key,))

    def finish(self):
        waits = []
        for s, tot in self.dcnt.items():
            if self.waited["sync"].get(("d", s), -1) < tot:
                waits.append((self.dsem[s], tot))
        for e in ENGS:
            if e != "sync" and self.cnt[e] > 0:
                if self.waited["sync"].get(("c", e), -1) < self.cnt[e]:
                    waits.append((self.chain[e], self.cnt[e]))

        def fn(e):
            return e.nop()
        self.ops["sync"].append((waits, fn, []))
        self._emit()

    def _emit(self):
        with self.nc.Block() as block:
            for e in ENGS:
                if not self.ops[e]:
                    continue

                def body(eng, e=e):
                    for waits, fn, incs in self.ops[e]:
                        for sem, val in waits:
                            eng.wait_ge(sem, val)
                        ins = fn(eng)
                        for sem, val in incs:
                            ins = ins.then_inc(sem, val)
                getattr(block, e)(body)
        self.es.__exit__(None, None, None)


def setup_consts(P):
    c = {}
    c["id_bf"] = P.sbuf("id_bf", [128, 128], BF16)
    c["id_f32"] = P.sbuf("id_f32", [128, 128], F32)
    c["iota_col"] = P.sbuf("iota_col", [128, 1], F32)
    c["iota_row"] = P.sbuf("iota_row", [128, 128], F32)
    c["ones_row_b"] = P.sbuf("ones_row_b", [128, 512], BF16)
    c["ones_row_f"] = P.sbuf("ones_row_f", [128, 512], F32)
    c["ones_col_b"] = P.sbuf("ones_col_b", [128, 1], BF16)
    c["ones_col_f"] = P.sbuf("ones_col_f", [128, 1], F32)

    a = c["iota_col"][:, 0:1]
    P.op("gpsimd", lambda e, a=a: e.iota(a, pattern=[[0, 1]], base=0,
         channel_multiplier=1, allow_small_or_imprecise_dtypes=True),
         writes=("iota_col",))
    a = c["iota_row"][:, :]
    P.op("gpsimd", lambda e, a=a: e.iota(a, pattern=[[1, 128]], base=0,
         channel_multiplier=0, allow_small_or_imprecise_dtypes=True),
         writes=("iota_row",))
    for nm in ("ones_row_b", "ones_row_f", "ones_col_b", "ones_col_f"):
        a = c[nm][:, :]
        P.op("gpsimd", lambda e, a=a: e.memset(a, 1.0), writes=(nm,))
    a, b_, d_ = c["id_f32"][:, :], c["iota_row"][:, :], c["iota_col"][:, 0:1]
    P.op("vector", lambda e, a=a, b=b_, d=d_: e.tensor_scalar(
        a, b, d, None, op0=ALU.is_equal),
        reads=("iota_row", "iota_col"), writes=("id_f32",))
    a, b_ = c["id_bf"][:, :], c["id_f32"][:, :]
    P.op("vector", lambda e, a=a, b=b_: e.tensor_copy(a, b),
         reads=("id_f32",), writes=("id_bf",))
    return c


def load_weight_bf16(P, sb, sb_key, hbm_ap, n_k, n_out, sem="dq_w"):
    v = sb[:, :].rearrange("p (k o) -> p k o", o=n_out)
    for kc in range(n_k):
        P.dma("gpsimd", v[:, kc, :], hbm_ap[kc * 128:(kc + 1) * 128, :],
              writes=(sb_key,), sem=sem)


def bcast_row(P, psum_ap, psum_key, row_ap, row_keys, consts, nparts,
              dtype="f", base=0, tp=None):
    rkey = "ones_row_b" if dtype == "b" else "ones_row_f"
    lhsT = consts[rkey][base:base + 1, 0:nparts]
    return P.matmuls(psum_key, [(lhsT, row_ap, psum_ap,
                                 tuple(row_keys) + (rkey,))], tp=tp)


def col2row(P, row_ap, row_key, col_ap, col_keys, consts, pt_ap, pt_key,
            n, dtype="f"):
    ident = consts["id_bf" if dtype == "b" else "id_f32"][:, :]
    ikey = "id_bf" if dtype == "b" else "id_f32"
    P.transpose(pt_ap, pt_key, col_ap, col_keys, ident, ikey)
    P.op("vector", lambda e, o=row_ap, i=pt_ap: e.tensor_copy(o, i),
         reads=(pt_key,), writes=(row_key,))


W0, C0, H0, DH0 = 256, 8, 16, 64


def ln_inplace(P, xv, x_key, stat, sq, sqk, pa, pb, mbc, rbc, consts):
    sqs = sq[:, 0:512]
    for nb in range(4):
        cols = slice(nb * 512, (nb + 1) * 512)
        sums = stat[0:1, 0:512]
        var = stat[0:1, 512:1024]
        tmp = stat[0:1, 1024:1536]
        mean, rstd = sums, var
        items = [(consts["ones_col_b"][:, 0:1], xv[:, ch, cols],
                  pa[0:1, 0:512], (x_key, "ones_col_b")) for ch in range(ND)]
        P.matmuls("pa", items)
        o, i = sums, pa[0:1, 0:512]
        P.op("vector", lambda e, o=o, i=i: e.tensor_copy(o, i),
             reads=("pa",), writes=("stat",))
        for ch in range(ND):
            o, i = sqs, xv[:, ch, cols]
            P.op("scalar", lambda e, o=o, i=i: e.activation(
                o, i, AF.Square), reads=(x_key,), writes=(sqk,))
            items = [(consts["ones_col_b"][:, 0:1], sqs, pb[0:1, 0:512],
                      (sqk, "ones_col_b"))]
            P.matmuls("pb", items, start=(ch == 0), stop=(ch == ND - 1))
        o, i = var, pb[0:1, 0:512]
        P.op("vector", lambda e, o=o, i=i: e.tensor_copy(o, i),
             reads=("pb",), writes=("stat",))
        P.op("vector", lambda e, o=mean, i=sums: e.tensor_scalar(
            o, i, 1.0 / D, None, op0=ALU.mult), reads=("stat",),
            writes=("stat",))
        P.op("vector", lambda e, o=tmp, i=mean: e.tensor_tensor(
            o, i, i, op=ALU.mult), reads=("stat",), writes=("stat",))
        P.op("vector", lambda e, o=var, i=var: e.tensor_scalar(
            o, i, 1.0 / D, None, op0=ALU.mult), reads=("stat",),
            writes=("stat",))
        P.op("vector", lambda e, o=var, a=var, b=tmp: e.tensor_tensor(
            o, a, b, op=ALU.subtract), reads=("stat",), writes=("stat",))
        P.op("vector", lambda e, o=var: e.tensor_scalar(
            o, o, 1e-5, None, op0=ALU.add), reads=("stat",),
            writes=("stat",))
        P.op("scalar", lambda e, o=rstd, i=var: e.activation(
            o, i, AF.Sqrt, bias=0.0, scale=1.0), reads=("stat",),
            writes=("stat",))
        P.op("vector", lambda e, o=rstd: e.reciprocal(o, o), reads=("stat",),
             writes=("stat",))
        bcast_row(P, pa[:, 0:512], "pa", mean, ("stat",), consts, 128)
        o, i = mbc[:, cols], pa[:, 0:512]
        P.op("vector", lambda e, o=o, i=i: e.tensor_copy(o, i),
             reads=("pa",), writes=("mbc",))
        bcast_row(P, pb[:, 0:512], "pb", rstd, ("stat",), consts, 128)
        o, i = rbc[:, cols], pb[:, 0:512]
        P.op("vector", lambda e, o=o, i=i: e.tensor_copy(o, i),
             reads=("pb",), writes=("rbc",))
    for ch in range(ND):
        eng = "vector" if ch % 2 == 0 else "gpsimd"
        o, m_ = xv[:, ch, :], mbc[:, :]
        P.op(eng, lambda e, o=o, m=m_: e.tensor_tensor(
            o, o, m, op=ALU.subtract), reads=(x_key, "mbc"), writes=(x_key,))
        r_ = rbc[:, :]
        P.op(eng, lambda e, o=o, r=r_: e.tensor_tensor(
            o, o, r, op=ALU.mult), reads=(x_key, "rbc"), writes=(x_key,))


def build_k1():
    P = Prog("k1")
    x0T = P.dram("x0T", [D, S], F32, kind="ExternalInput").ap()
    wq = P.dram("wqT", [D, D], F32, kind="ExternalInput").ap()
    wk = P.dram("wkT", [D, D], F32, kind="ExternalInput").ap()
    wv = P.dram("wvT", [D, D], F32, kind="ExternalInput").ap()
    bq = P.dram("bq", [1, D], F32, kind="ExternalInput").ap()
    bk = P.dram("bk", [1, D], F32, kind="ExternalInput").ap()
    bv = P.dram("bv", [1, D], F32, kind="ExternalInput").ap()
    ctxT_o = P.dram("ctxT", [D, S], BF16, kind="ExternalOutput").ap()

    xln = P.sbuf("xln", [128, ND * S], BF16)
    qk = P.sbuf("qk", [128, 16 * S], BF16)
    vt = P.sbuf("vt", [128, NT * D], BF16)
    wb = P.sbuf("wb", [128, 8 * 1024], BF16)
    br = P.sbuf("br", [1, 3 * 1024], BF16)
    probs = P.sbuf("probs", [128, 1536], BF16)
    masks = P.sbuf("masks", [128, 1024], BF16)
    rrow = P.sbuf("rrow", [1, 256], F32)
    rbx = P.sbuf("rbx", [128, 256], F32)
    cst = P.sbuf("cst", [128, 4 * 256], BF16)
    stat = P.sbuf("stat", [128, S], F32)
    mbc = P.sbuf("mbc", [128, S], BF16)
    rbc = P.sbuf("rbc", [128, S], BF16)
    sq = P.sbuf("sq", [128, 512], BF16)

    pa = P.psum("pa", [128, 512], F32)
    pb = P.psum("pb", [128, 512], F32)
    ps = P.psum("ps", [128, 1536], F32)
    pc = P.psum("pc", [128, 512], F32)
    pt = P.psum("pt", [128, 512], F32)

    consts = setup_consts(P)

    xv = xln[:, :].rearrange("p (c t) -> p c t", t=S)
    qkv_ = qk[:, :].rearrange("p (c t) -> p c t", t=S)
    vtv = vt[:, :].rearrange("p (t d) -> p t d", d=D)
    wbv = wb[:, :].rearrange("p (k o) -> p k o", o=1024)
    brv = br[0:1, :].rearrange("p (i o) -> p i o", o=1024)
    pr3 = probs[:, :].rearrange("p (b q) -> p b q", q=256)

    for ch in range(ND):
        P.dma("gpsimd", xv[:, ch, :], x0T[ch * 128:(ch + 1) * 128, :],
              writes=("xln",), sem="dq_x")
    for i, bsrc in enumerate((bq, bk, bv)):
        P.dma("gpsimd", brv[:, i, :], bsrc, writes=("br",), sem="w6")

    a = probs[:, 0:512]
    P.op("gpsimd", lambda e, a=a: e.memset(a, 1.0), writes=("probs",))
    otv = probs[:, 0:512].rearrange("p (t q) -> p t q", q=256)
    mvL = masks[:, 0:512].rearrange("p (t q) -> p t q", q=256)
    mvR = masks[:, 512:1024].rearrange("p (t q) -> p t q", q=256)
    srcv0 = otv[:, 0:2, :]
    P.op("gpsimd", lambda e, o=mvL, s=srcv0: e.affine_select(
        o, s, pattern=[[128, 2], [-1, 256]], compare_op=ALU.is_ge,
        fill=0.0, base=0, channel_multiplier=1),
        reads=("probs",), writes=("masks",))
    P.op("gpsimd", lambda e, o=mvR, s=srcv0: e.affine_select(
        o, s, pattern=[[-128, 2], [1, 256]], compare_op=ALU.is_ge,
        fill=0.0, base=0, channel_multiplier=-1),
        reads=("probs",), writes=("masks",))

    ln_inplace(P, xv, "xln", stat, sq, "sq", pa, pb, mbc, rbc, consts)

    def proj_featmajor(wsrc, bias_idx, out_slot0, scale, wsem):
        load_weight_bf16(P, wb, "wb", wsrc, ND, 1024, sem=wsem)
        for oc in range(ND):
            for nb in range(4):
                cols = slice(nb * 512, (nb + 1) * 512)
                items = [(wbv[:, kc, oc * 128:(oc + 1) * 128],
                          xv[:, kc, cols], pa[:, 0:512], ("wb", "xln"))
                         for kc in range(ND)]
                items.append((brv[0:1, bias_idx, oc * 128:(oc + 1) * 128],
                              consts["ones_row_b"][0:1, 0:512], pa[:, 0:512],
                              ("br", "ones_row_b")))
                P.matmuls("pa", items)
                o, i = qkv_[:, out_slot0 + oc, cols], pa[:, 0:512]
                P.op("scalar", lambda e, o=o, i=i, s=scale: e.activation(
                    o, i, AF.Copy, bias=0.0, scale=s),
                    reads=("pa",), writes=("qk",))

    proj_featmajor(wq, 0, 0, 0.125, "w0")
    proj_featmajor(wk, 1, 8, 1.0, "w1")

    load_weight_bf16(P, wb, "wb", wv, ND, 1024, sem="w2")
    for tc in range(NT):
        for nb in range(2):
            cols = slice(nb * 512, (nb + 1) * 512)
            items = [(xv[:, kc, tc * 128:(tc + 1) * 128],
                      wbv[:, kc, cols], pb[:, 0:512], ("wb", "xln"))
                     for kc in range(ND)]
            items.append((consts["ones_row_b"][0:1, 0:128],
                          brv[0:1, 2, cols], pb[:, 0:512],
                          ("br", "ones_row_b")))
            P.matmuls("pb", items)
            o, i = vtv[:, tc, cols], pb[:, 0:512]
            P.op("vector", lambda e, o=o, i=i: e.tensor_copy(o, i),
                 reads=("pb",), writes=("vt",))

    for h in range(H0):
        qtile, qbase = h // 2, (h % 2) * 64
        for ci in range(C0):
            qs = slice(ci * 256, ci * 256 + 256)
            items = []
            for kb in range(6):
                tb = min(max(2 * (ci - 1) + kb, 0), NT - 1)
                lhsT = qkv_[qbase:qbase + 64, 8 + qtile,
                            tb * 128:(tb + 1) * 128]
                rhs = qkv_[qbase:qbase + 64, qtile, qs]
                items.append((lhsT, rhs, ps[:, kb * 256:(kb + 1) * 256],
                              ("qk",)))
            P.matmuls("ps", items, each=True)
            o, i = probs[:, :], ps[:, :]
            P.op("scalar", lambda e, o=o, i=i: e.activation(
                o, i, AF.Exp, bias=0.0, scale=1.0),
                reads=("ps",), writes=("probs",))
            o = probs[:, 0:512]
            if ci == 0:
                P.op("gpsimd", lambda e, o=o: e.memset(o, 0.0),
                     reads=("probs",), writes=("probs",))
            else:
                m_ = masks[:, 0:512]
                P.op("gpsimd", lambda e, o=o, m=m_: e.tensor_tensor(
                    o, o, m, op=ALU.mult), reads=("probs", "masks"),
                    writes=("probs",))
            o = probs[:, 1024:1536]
            if ci == C0 - 1:
                P.op("gpsimd", lambda e, o=o: e.memset(o, 0.0),
                     reads=("probs",), writes=("probs",))
            else:
                m_ = masks[:, 512:1024]
                P.op("gpsimd", lambda e, o=o, m=m_: e.tensor_tensor(
                    o, o, m, op=ALU.mult), reads=("probs", "masks"),
                    writes=("probs",))
            items = [(consts["ones_col_b"][:, 0:1], pr3[:, kb, :],
                      pc[0:1, 0:256], ("probs", "ones_col_b"))
                     for kb in range(6)]
            P.matmuls("pc", items)
            o, i = rrow[0:1, :], pc[0:1, 0:256]
            P.op("vector", lambda e, o=o, i=i: e.reciprocal(o, i),
                 reads=("pc",), writes=("rrow",))
            items = []
            for kb in range(6):
                tb = min(max(2 * (ci - 1) + kb, 0), NT - 1)
                items.append((vtv[:, tb, h * 64:(h + 1) * 64],
                              pr3[:, kb, :], pt[qbase:qbase + 64, 0:256],
                              ("vt", "probs")))
            P.matmuls("pt", items)
            bcast_row(P, pc[qbase:qbase + 64, 0:256], "pc", rrow[0:1, :],
                      ("rrow",), consts, 64)
            o, i = rbx[qbase:qbase + 64, :], pc[qbase:qbase + 64, 0:256]
            P.op("vector", lambda e, o=o, i=i: e.tensor_copy(o, i),
                 reads=("pc",), writes=("rbx",))
            slot = (h * C0 + ci) % 4
            ck = f"cst{slot}"
            o = cst[qbase:qbase + 64, slot * 256:(slot + 1) * 256]
            a_, b_ = pt[qbase:qbase + 64, 0:256], rbx[qbase:qbase + 64, :]
            P.op("vector", lambda e, o=o, a=a_, b=b_: e.tensor_tensor(
                o, a, b, op=ALU.mult), reads=("pt", "rbx"), writes=(ck,))
            dst = ctxT_o[h * 64:(h + 1) * 64, qs]
            P.dma("sync", dst, o, reads=(ck,), sem=f"ctxo{slot}")

    P.finish()
    return P.nc


def build_k2():
    P = Prog("k2")
    x0T = P.dram("x0T", [D, S], F32, kind="ExternalInput").ap()
    ctxT = P.dram("ctxT", [D, S], BF16, kind="ExternalInput").ap()
    wr = P.dram("wrT", [D, 8], F32, kind="ExternalInput").ap()
    wg = P.dram("wgT", [D, 2048], F32, kind="ExternalInput").ap()

    x1T_o = P.dram("x1T", [D, S], BF16, kind="ExternalOutput").ap()
    hT_o = P.dram("hT", [2048, S], BF16, kind="ExternalOutput").ap()
    idx_o = P.dram("idx2", [128, 2 * NT], F32, kind="ExternalOutput").ap()
    gat_o = P.dram("gat2", [128, 2 * NT], F32, kind="ExternalOutput").ap()
    pos_o = P.dram("pos2", [128, 2 * NT], F32, kind="ExternalOutput").ap()
    cnt_o = P.dram("cnt", [8, 1], F32, kind="ExternalOutput").ap()

    xs = P.sbuf("xs", [128, ND * S], BF16)
    cstg = P.sbuf("cstg", [128, ND * S], BF16)
    hstg = P.sbuf("hstg", [128, 16 * S], BF16)
    wb = P.sbuf("wb", [128, 8 * 1024], BF16)
    stat = P.sbuf("stat", [128, S], F32)
    mbc = P.sbuf("mbc", [128, S], BF16)
    rbc = P.sbuf("rbc", [128, S], BF16)
    sq = P.sbuf("sq", [128, 512], BF16)
    lgt = P.sbuf("lgt", [8, S], BF16)
    ltok = P.sbuf("ltok", [128, NT * 8], F32)
    rt = P.sbuf("rt", [128, 64], F32)
    gts = P.sbuf("gts", [128, NT * 8], F32)
    sel = P.sbuf("sel", [128, NT * 8], F32)
    ohk = P.sbuf("ohk", [128, 2 * NT * 8], F32)
    mtok = P.sbuf("mtok", [128, NT * 8], BF16)
    cexT = P.sbuf("cexT", [8, S], F32)
    ctok = P.sbuf("ctok", [128, NT * 8], F32)
    io8 = P.sbuf("io8", [128, NT * 8], F32)
    ioB = P.sbuf("ioB", [128, NT * 8], F32)
    tmp3b = P.sbuf("tmp3b", [128, NT * 8], F32)
    idxk_sb = P.sbuf("idxk_sb", [128, 2 * NT], F32)
    gk_sb = P.sbuf("gk_sb", [128, 2 * NT], F32)
    posk_sb = P.sbuf("posk_sb", [128, 2 * NT], F32)
    wrb = P.sbuf("wrb", [128, 8 * 8], BF16)

    pa = P.psum("pa", [128, 512], F32)
    pb = P.psum("pb", [128, 512], F32)
    pc = P.psum("pc", [128, 512], F32)
    pt = P.psum("pt", [128, 512], F32)
    ptb = P.psum("ptb", [128, 128], BF16)

    consts = setup_consts(P)
    ID_B = consts["id_bf"][:, :]
    ID_F8 = consts["id_f32"][0:8, 0:8]

    xv = xs[:, :].rearrange("p (c t) -> p c t", t=S)
    wbv = wb[:, :].rearrange("p (k o) -> p k o", o=1024)

    cv = cstg[:, :].rearrange("p (c t) -> p c t", t=S)
    for ch in range(ND):
        P.dma("gpsimd", xv[:, ch, :], x0T[ch * 128:(ch + 1) * 128, :],
              writes=("xs",), sem="dq_x")
        P.dma("sync", cv[:, ch, :], ctxT[ch * 128:(ch + 1) * 128, :],
              writes=("cstg",), sem="dq_c")
    for ch in range(ND):
        o, b_ = xv[:, ch, :], cv[:, ch, :]
        P.op("vector", lambda e, o=o, b=b_: e.tensor_tensor(
            o, o, b, op=ALU.add), reads=("xs", "cstg"), writes=("xs",))
        P.dma("sync", x1T_o[ch * 128:(ch + 1) * 128, :], o,
              reads=("xs",), sem="dq_out")

    ln_inplace(P, xv, "xs", stat, sq, "sq", pa, pb, mbc, rbc, consts)

    hvv = hstg[:, :].rearrange("p (c t) -> p c t", t=S)
    for half in range(2):
        for kc in range(ND):
            o = wbv[:, kc, :]
            i = wg[kc * 128:(kc + 1) * 128, half * 1024:(half + 1) * 1024]
            P.dma("gpsimd", o, i, writes=("wb",), sem=f"w{3 + half}")
        for oc in range(8):
            row = half * 8 + oc
            for nb in range(4):
                cols = slice(nb * 512, (nb + 1) * 512)
                items = [(wbv[:, kc, oc * 128:(oc + 1) * 128],
                          xv[:, kc, cols], pa[:, 0:512], ("wb", "xs"))
                         for kc in range(ND)]
                P.matmuls("pa", items)
                o, i = hvv[:, row, cols], pa[:, 0:512]
                P.op("scalar", lambda e, o=o, i=i: e.activation(
                    o, i, AF.Silu), reads=("pa",), writes=("hstg",))
            P.dma("sync", hT_o[row * 128:(row + 1) * 128, :], hvv[:, row, :],
                  reads=("hstg",), sem="dq_out")

    wrbv = wrb[:, :].rearrange("p (k o) -> p k o", o=8)
    for kc in range(ND):
        P.dma("gpsimd", wrbv[:, kc, :], wr[kc * 128:(kc + 1) * 128, :],
              writes=("wrb",), sem="w5")
    for nb in range(4):
        cols = slice(nb * 512, (nb + 1) * 512)
        items = [(wrbv[:, kc, :], xv[:, kc, cols], pc[0:8, 0:512],
                  ("wrb", "xs")) for kc in range(ND)]
        P.matmuls("pc", items)
        o, i = lgt[:, cols], pc[0:8, 0:512]
        P.op("vector", lambda e, o=o, i=i: e.tensor_copy(o, i),
             reads=("pc",), writes=("lgt",))
    l3 = ltok[:, :].rearrange("p (j e) -> p j e", e=8)
    ID_B8 = consts["id_bf"][0:8, 0:8]
    for j in range(NT):
        P.transpose(ptb[:, 0:8], "ptb", lgt[0:8, j * 128:(j + 1) * 128],
                    ("lgt",), ID_B8, "id_bf")
        o, i = l3[:, j, :], ptb[:, 0:8]
        P.op("vector", lambda e, o=o, i=i: e.tensor_copy(o, i),
             reads=("ptb",), writes=("ltok",))

    g3 = gts[:, :].rearrange("p (j e) -> p j e", e=8)
    s3 = sel[:, :].rearrange("p (j e) -> p j e", e=8)
    mrow = rt[:, 0:NT]
    srow = rt[:, NT:2 * NT]
    mrow3 = mrow.rearrange("p (j u) -> p j u", u=1).broadcast_to([128, NT, 8])
    srow3 = srow.rearrange("p (j u) -> p j u", u=1).broadcast_to([128, NT, 8])
    P.op("vector", lambda e, o=mrow, i=l3: e.reduce_max(o, i, axis=AX.X),
         reads=("ltok",), writes=("rt",))
    P.op("vector", lambda e, o=g3, a=l3, b=mrow3: e.tensor_tensor(
        o, a, b, op=ALU.subtract), reads=("ltok", "rt"), writes=("gts",))
    a = gts[:, :]
    P.op("scalar", lambda e, a=a: e.activation(a, a, AF.Exp),
         reads=("gts",), writes=("gts",))
    P.op("vector", lambda e, o=srow, i=g3: e.reduce_sum(o, i, axis=AX.X),
         reads=("gts",), writes=("rt",))
    P.op("vector", lambda e, o=srow: e.reciprocal(o, o), reads=("rt",),
         writes=("rt",))
    P.op("vector", lambda e, o=g3, b=srow3: e.tensor_tensor(
        o, o, b, op=ALU.mult), reads=("gts", "rt"), writes=("gts",))
    a, b_ = sel[:, :], gts[:, :]
    P.op("vector", lambda e, a=a, b=b_: e.tensor_copy(a, b),
         reads=("gts",), writes=("sel",))

    a = io8[:, :]
    P.op("gpsimd", lambda e, a=a: e.iota(a, pattern=[[0, NT], [1, 8]],
         base=0, channel_multiplier=0, allow_small_or_imprecise_dtypes=True),
         writes=("io8",))
    a, b_ = ioB[:, :], io8[:, :]
    P.op("vector", lambda e, a=a, b=b_: e.tensor_scalar(
        a, b, -1024.0, None, op0=ALU.add), reads=("io8",), writes=("ioB",))

    oh3 = ohk[:, :].rearrange("p (k j e) -> p k j e", k=2, e=8)
    iov = io8[:, :].rearrange("p (j e) -> p j e", e=8)
    iobv = ioB[:, :].rearrange("p (j e) -> p j e", e=8)
    iv = idxk_sb[:, :].rearrange("p (k j) -> p k j", k=2)
    gv = gk_sb[:, :].rearrange("p (k j) -> p k j", k=2)
    pv = posk_sb[:, :].rearrange("p (k j) -> p k j", k=2)
    t3 = tmp3b[:, :].rearrange("p (j e) -> p j e", e=8)

    for k in range(2):
        P.op("vector", lambda e, o=mrow, i=s3: e.reduce_max(o, i, axis=AX.X),
             reads=("sel",), writes=("rt",))
        P.op("vector", lambda e, o=t3, a=s3, b=mrow3: e.tensor_tensor(
            o, a, b, op=ALU.is_equal), reads=("sel", "rt"), writes=("tmp3b",))
        P.op("vector", lambda e, o=t3, b=iobv: e.tensor_tensor(
            o, o, b, op=ALU.mult), reads=("tmp3b", "ioB"), writes=("tmp3b",))
        o = iv[:, k, :]
        P.op("vector", lambda e, o=o, i=t3: e.tensor_reduce(
            o, i, axis=AX.X, op=ALU.min), reads=("tmp3b",),
            writes=("idxk_sb",))
        P.op("vector", lambda e, o=o: e.tensor_scalar(
            o, o, 1024.0, None, op0=ALU.add), reads=("idxk_sb",),
            writes=("idxk_sb",))
        ob = oh3[:, k, :, :]
        ib3 = iv[:, k, :].rearrange("p (j u) -> p j u", u=1) \
            .broadcast_to([128, NT, 8])
        P.op("vector", lambda e, o=ob, a=iov, b=ib3: e.tensor_tensor(
            o, a, b, op=ALU.is_equal), reads=("io8", "idxk_sb"),
            writes=("ohk",))
        P.op("vector", lambda e, o=t3, a=ob, b=g3: e.tensor_tensor(
            o, a, b, op=ALU.mult), reads=("ohk", "gts"), writes=("tmp3b",))
        o = gv[:, k, :]
        P.op("vector", lambda e, o=o, i=t3: e.reduce_sum(o, i, axis=AX.X),
             reads=("tmp3b",), writes=("gk_sb",))
        P.op("vector", lambda e, o=s3, a=ob: e.scalar_tensor_tensor(
            o, a, -1024.0, o, op0=ALU.mult, op1=ALU.add),
            reads=("ohk", "sel"), writes=("sel",))

    m3 = mtok[:, :].rearrange("p (j e) -> p j e", e=8)
    P.op("vector", lambda e, o=m3, a=oh3: e.tensor_tensor(
        o, a[:, 0, :, :], a[:, 1, :, :], op=ALU.add),
        reads=("ohk",), writes=("mtok",))
    maskT = lgt
    for j in range(NT):
        P.transpose(ptb[0:8, 0:128], "ptb", m3[:, j, :], ("mtok",),
                    ID_B, "id_bf")
        o, i = maskT[0:8, j * 128:(j + 1) * 128], ptb[0:8, 0:128]
        P.op("vector", lambda e, o=o, i=i: e.tensor_copy(o, i),
             reads=("ptb",), writes=("lgt",))
    a, b_ = cexT[0:8, :], maskT[0:8, :]
    P.op("vector", lambda e, o=a, m=b_: e.tensor_tensor_scan(
        o, m, m, 0.0, op0=ALU.add, op1=ALU.bypass),
        reads=("lgt",), writes=("cexT",))
    o, i = rt[0:8, 32:33], cexT[0:8, S - 1:S]
    P.op("vector", lambda e, o=o, i=i: e.tensor_copy(o, i),
         reads=("cexT",), writes=("rt",))
    P.dma("sync", cnt_o[:, :], rt[0:8, 32:33], reads=("rt",), sem="dq_out")
    P.op("vector", lambda e, o=a, m=b_: e.tensor_tensor(
        o, o, m, op=ALU.subtract), reads=("cexT", "lgt"), writes=("cexT",))
    c3 = ctok[:, :].rearrange("p (j e) -> p j e", e=8)
    for j in range(NT):
        P.transpose(pt[:, 0:8], "pt", cexT[0:8, j * 128:(j + 1) * 128],
                    ("cexT",), ID_F8, "id_f32")
        o, i = c3[:, j, :], pt[:, 0:8]
        P.op("vector", lambda e, o=o, i=i: e.tensor_copy(o, i),
             reads=("pt",), writes=("ctok",))
    for k in range(2):
        a = oh3[:, k, :, :]
        P.op("vector", lambda e, o=t3, a=a, b=c3: e.tensor_tensor(
            o, a, b, op=ALU.mult), reads=("ohk", "ctok"), writes=("tmp3b",))
        o = pv[:, k, :]
        P.op("vector", lambda e, o=o, i=t3: e.reduce_sum(o, i, axis=AX.X),
             reads=("tmp3b",), writes=("posk_sb",))

    P.dma("sync", idx_o[:, :], idxk_sb[:, :], reads=("idxk_sb",),
          sem="dq_out")
    P.dma("sync", gat_o[:, :], gk_sb[:, :], reads=("gk_sb",), sem="dq_out")
    P.dma("sync", pos_o[:, :], posk_sb[:, :], reads=("posk_sb",),
          sem="dq_out")

    P.finish()
    return P.nc




CAP0 = 2560
NSLOT0 = 8 * CAP0
NREC0 = NSLOT0 + 128


def build_k3():
    P = Prog("k3")
    idx_i = P.dram("idx2", [128, 2 * NT], F32, kind="ExternalInput").ap()
    gat_i = P.dram("gat2", [128, 2 * NT], F32, kind="ExternalInput").ap()
    pos_i = P.dram("pos2", [128, 2 * NT], F32, kind="ExternalInput").ap()
    cnts_i = P.dram("cnts", [8, 8], F32, kind="ExternalInput").ap()
    rec_o = P.dram("rec", [NREC0, 1], F32, kind="ExternalOutput").ap()
    slot_o = P.dram("slot2", [128, 2 * NT], F32, kind="ExternalOutput").ap()
    gn_o = P.dram("gn2", [128, 2 * NT], F32, kind="ExternalOutput").ap()
    pid_t = P.nc.partition_id_tensor.ap()

    sio = P.sbuf("sio", [128, 2 * NT], F32)
    sga = P.sbuf("sga", [128, 2 * NT], F32)
    spo = P.sbuf("spo", [128, 2 * NT], F32)
    scn = P.sbuf("scn", [8, 8], F32)
    spid = P.sbuf("spid", [1, 1], mybir.dt.uint32)
    spidf = P.sbuf("spidf", [1, 1], F32)
    smk = P.sbuf("smk", [8, 1], F32)
    sprefix = P.sbuf("sprefix", [1, 8], F32)
    sprefb = P.sbuf("sprefb", [128, 8], F32)
    spidc = P.sbuf("spidc", [128, 1], F32)
    io8 = P.sbuf("io8", [128, NT * 8], F32)
    iot = P.sbuf("iot", [128, NT], F32)
    t3a = P.sbuf("t3a", [128, NT * 8], F32)
    gp = P.sbuf("gp", [128, 2 * NT], F32)
    kp = P.sbuf("kp", [128, 2 * NT], F32)
    sl = P.sbuf("sl", [128, 2 * NT], F32)
    gs = P.sbuf("gs", [128, NT], F32)
    tk = P.sbuf("tk", [128, NT], F32)
    sli = P.sbuf("sli", [128, 2 * NT], I32)
    zero = P.sbuf("zero", [128, 512], F32)

    pa = P.psum("pa", [128, 512], F32)
    pb = P.psum("pb", [128, 512], F32)

    consts = setup_consts(P)

    P.dma("sync", sio[:, :], idx_i, writes=("sio",), sem="d0")
    P.dma("sync", sga[:, :], gat_i, writes=("sga",), sem="d0")
    P.dma("sync", spo[:, :], pos_i, writes=("spo",), sem="d0")
    P.dma("sync", scn[:, :], cnts_i, writes=("scn",), sem="d0")
    P.dma("sync", spid[:, :], pid_t, writes=("spid",), sem="d0")

    a = zero[:, :]
    P.op("gpsimd", lambda e, a=a: e.memset(a, 0.0), writes=("zero",))
    recv = rec_o[0:NREC0, :].rearrange("(p r) c -> p (r c)", p=128)
    P.dma("gpsimd", recv, zero[:, 0:NREC0 // 128], reads=("zero",),
          writes=("rec",), sem="dz")
    sent = P.sbuf("sent", [128, 2 * NT], F32)
    a = sent[:, :]
    P.op("gpsimd", lambda e, a=a: e.iota(a, pattern=[[1, 2 * NT]],
         base=NSLOT0, channel_multiplier=0,
         allow_small_or_imprecise_dtypes=True), writes=("sent",))

    o, i = spidf[:, :], spid[:, :]
    P.op("vector", lambda e, o=o, i=i: e.tensor_copy(o, i),
         reads=("spid",), writes=("spidf",))
    bcast_row(P, pa[0:8, 0:1], "pa", spidf[0:1, 0:1], ("spidf",), consts, 8)
    o, i = smk[:, :], pa[0:8, 0:1]
    P.op("vector", lambda e, o=o, i=i: e.tensor_copy(o, i), reads=("pa",),
         writes=("smk",))
    a, b_ = smk[:, :], consts["iota_col"][0:8, 0:1]
    P.op("vector", lambda e, o=a, b=b_: e.tensor_tensor(
        o, b, o, op=ALU.is_lt), reads=("smk", "iota_col"), writes=("smk",))
    P.matmuls("pb", [(smk[:, :], scn[:, :], pb[0:1, 0:8], ("smk", "scn"))])
    o, i = sprefix[:, :], pb[0:1, 0:8]
    P.op("vector", lambda e, o=o, i=i: e.tensor_copy(o, i), reads=("pb",),
         writes=("sprefix",))
    bcast_row(P, pa[:, 0:8], "pa", sprefix[0:1, :], ("sprefix",), consts, 128)
    o, i = sprefb[:, :], pa[:, 0:8]
    P.op("vector", lambda e, o=o, i=i: e.tensor_copy(o, i), reads=("pa",),
         writes=("sprefb",))
    bcast_row(P, pb[:, 0:1], "pb", spidf[0:1, 0:1], ("spidf",), consts, 128)
    o, i = spidc[:, :], pb[:, 0:1]
    P.op("vector", lambda e, o=o, i=i: e.tensor_copy(o, i), reads=("pb",),
         writes=("spidc",))

    a = io8[:, :]
    P.op("gpsimd", lambda e, a=a: e.iota(a, pattern=[[0, NT], [1, 8]],
         base=0, channel_multiplier=0, allow_small_or_imprecise_dtypes=True),
         writes=("io8",))
    a = iot[:, :]
    P.op("gpsimd", lambda e, a=a: e.iota(a, pattern=[[128, NT]], base=0,
         channel_multiplier=1, allow_small_or_imprecise_dtypes=True),
         writes=("iot",))
    o = spidc[:, 0:1]
    P.op("vector", lambda e, o=o: e.tensor_scalar(
        o, o, 2048.0, None, op0=ALU.mult), reads=("spidc",),
        writes=("spidc",))
    o, a, b_ = tk[:, :], iot[:, :], spidc[:, 0:1]
    P.op("vector", lambda e, o=o, a=a, b=b_: e.tensor_scalar(
        o, a, b, None, op0=ALU.add), reads=("iot", "spidc"), writes=("tk",))

    iov = io8[:, :].rearrange("p (j e) -> p j e", e=8)
    t3 = t3a[:, :].rearrange("p (j e) -> p j e", e=8)
    prefb3 = sprefb[:, :].rearrange("p (u e) -> p u e", u=1) \
        .broadcast_to([128, NT, 8])
    iv = sio[:, :].rearrange("p (k j) -> p k j", k=2)
    gv = sga[:, :].rearrange("p (k j) -> p k j", k=2)
    pv = spo[:, :].rearrange("p (k j) -> p k j", k=2)
    gpv = gp[:, :].rearrange("p (k j) -> p k j", k=2)
    kpv = kp[:, :].rearrange("p (k j) -> p k j", k=2)
    slv = sl[:, :].rearrange("p (k j) -> p k j", k=2)

    for k in range(2):
        ib3 = iv[:, k, :].rearrange("p (j u) -> p j u", u=1) \
            .broadcast_to([128, NT, 8])
        P.op("vector", lambda e, o=t3, a=iov, b=ib3: e.tensor_tensor(
            o, a, b, op=ALU.is_equal), reads=("io8", "sio"), writes=("t3a",))
        P.op("vector", lambda e, o=t3, b=prefb3: e.tensor_tensor(
            o, o, b, op=ALU.mult), reads=("t3a", "sprefb"), writes=("t3a",))
        o = gpv[:, k, :]
        P.op("vector", lambda e, o=o, i=t3: e.reduce_sum(o, i, axis=AX.X),
             reads=("t3a",), writes=("gp",))
        a = pv[:, k, :]
        P.op("vector", lambda e, o=o, a=a: e.tensor_tensor(
            o, o, a, op=ALU.add), reads=("gp", "spo"), writes=("gp",))
        ok_ = kpv[:, k, :]
        P.op("vector", lambda e, o=ok_, a=o: e.tensor_scalar(
            o, a, float(CAP0), None, op0=ALU.is_lt), reads=("gp",),
            writes=("kp",))
        sv = sent[:, :].rearrange("p (k j) -> p k j", k=2)[:, k, :]
        os_ = slv[:, k, :]
        a = iv[:, k, :]
        P.op("vector", lambda e, o=os_, a=a: e.tensor_scalar(
            o, a, float(CAP0), None, op0=ALU.mult), reads=("sio",),
            writes=("sl",))
        P.op("vector", lambda e, o=os_, b=o: e.tensor_tensor(
            o, o, b, op=ALU.add), reads=("sl", "gp"), writes=("sl",))
        P.op("vector", lambda e, o=os_, b=sv: e.tensor_tensor(
            o, o, b, op=ALU.subtract), reads=("sl", "sent"), writes=("sl",))
        P.op("vector", lambda e, o=os_, b=ok_: e.tensor_tensor(
            o, o, b, op=ALU.mult), reads=("sl", "kp"), writes=("sl",))
        P.op("vector", lambda e, o=os_, b=sv: e.tensor_tensor(
            o, o, b, op=ALU.add), reads=("sl", "sent"), writes=("sl",))
        og = gv[:, k, :]
        P.op("vector", lambda e, o=og, b=ok_: e.tensor_tensor(
            o, o, b, op=ALU.mult), reads=("sga", "kp"), writes=("sga",))

    o = gs[:, :]
    P.op("vector", lambda e, o=o, a=gv: e.tensor_tensor(
        o, a[:, 0, :], a[:, 1, :], op=ALU.add), reads=("sga",), writes=("gs",))
    P.op("vector", lambda e, o=o: e.tensor_scalar(
        o, o, 1e-6, None, op0=ALU.add), reads=("gs",), writes=("gs",))
    P.op("vector", lambda e, o=o: e.reciprocal(o, o), reads=("gs",),
         writes=("gs",))
    for k in range(2):
        og = gv[:, k, :]
        P.op("vector", lambda e, o=og, b=o: e.tensor_tensor(
            o, o, b, op=ALU.mult), reads=("sga", "gs"), writes=("sga",))

    o, i = sli[:, :], sl[:, :]
    P.op("vector", lambda e, o=o, i=i: e.tensor_copy(o, i), reads=("sl",),
         writes=("sli",))
    sliv = sli[:, :].rearrange("p (k j) -> p k j", k=2)
    for k in range(2):
        for j in range(NT):
            src = tk[:, j:j + 1]
            off = sliv[:, k, j:j + 1]
            P.idma(rec_o, IndirectOffsetOnAxis(off, axis=0), src, None,
                   reads=("tk", "sli", "rec"), sem=f"ix{(k * NT + j) % 4}")

    P.dma("sync", slot_o, sl[:, :], reads=("sl",), sem="dq_out")
    P.dma("sync", gn_o, sga[:, :], reads=("sga",), sem="dq_out")
    P.finish()
    return P.nc


def build_k4():
    P = Prog("k4")
    recs = P.dram("recs", [8, CAP0, 1], F32, kind="ExternalInput").ap()
    htok = P.dram("htok", [16384, 2048], BF16, kind="ExternalInput").ap()
    wdt = P.dram("wdTe", [2048, 1024], F32, kind="ExternalInput").ap()
    yw_o = P.dram("yw", [CAP0, 1024], BF16, kind="ExternalOutput").ap()

    recs_hbm = P.dram("recs_hbm", [CAP0, 1], F32, kind="Internal").ap()
    wds = P.sbuf("wds", [128, 16 * 1024], BF16)
    rsum = P.sbuf("rsum", [128, 8 * 20], F32)
    rtmp = P.sbuf("rtmp", [128, 20], F32)
    rint = P.sbuf("rint", [128, 20], I32)
    rcol = P.sbuf("rcol", [128, 20], F32)
    hs = P.sbuf("hs", [128, 2048], BF16)
    hsT = P.sbuf("hsT", [128, 16 * 512], BF16)
    yT = P.sbuf("yT", [128, 8 * 512], BF16)
    ywst = P.sbuf("ywst", [128, 4 * 1024], BF16)

    pa = P.psum("pa", [128, 512], F32)
    ptb = P.psum("ptb", [128, 512], BF16)

    consts = setup_consts(P)
    ID_B = consts["id_bf"][:, :]

    rv = rsum[:, :].rearrange("p (c r) -> p c r", r=20)
    for c in range(8):
        vsl = recs[c, :, :].rearrange("(p r) u -> p (r u)", p=128)
        P.dma("sync", rv[:, c, :], vsl, writes=("rsum",), sem="d0")
    o = rtmp[:, :]
    P.op("vector", lambda e, o=o, a=rv: e.tensor_copy(o, a[:, 0, :]),
         reads=("rsum",), writes=("rtmp",))
    for c in range(1, 8):
        P.op("vector", lambda e, o=o, a=rv, c=c: e.tensor_tensor(
            o, o, a[:, c, :], op=ALU.add), reads=("rtmp", "rsum"),
            writes=("rtmp",))
    rb = recs_hbm.rearrange("(p r) u -> p (r u)", p=128)
    P.dma("sync", rb, rtmp[:, :], reads=("rtmp",), writes=("recs_hbm",),
          sem="d1")
    rcv = rcol[:, :]
    for r in range(20):
        P.dma("sync", rcv[:, r:r + 1],
              recs_hbm[r * 128:(r + 1) * 128, :],
              reads=("recs_hbm",), writes=("rcol",), sem="d2")
    o, i = rint[:, :], rcol[:, :]
    P.op("vector", lambda e, o=o, i=i: e.tensor_copy(o, i), reads=("rcol",),
         writes=("rint",))

    wdv = wds[:, :].rearrange("p (k o) -> p k o", o=1024)
    for kc in range(16):
        P.dma("gpsimd", wdv[:, kc, :], wdt[kc * 128:(kc + 1) * 128, :],
              writes=("wds",), sem="w0")

    hsv = hs[:, :]
    hTv = hsT[:, :].rearrange("p (k s) -> p k s", s=512)
    yTv = yT[:, :].rearrange("p (o s) -> p o s", s=512)
    ywv = ywst[:, :].rearrange("p (j o) -> p j o", o=1024)

    for g in range(5):
        for j in range(4):
            r = g * 4 + j
            off = rint[:, r:r + 1]
            P.idma(hsv, None, htok, IndirectOffsetOnAxis(off, axis=0),
                   reads=("rint",), writes=("hs",), sem=f"ix{j % 2}")
            for kc in range(16):
                P.transpose(ptb[:, 0:128], "ptb",
                            hsv[:, kc * 128:(kc + 1) * 128], ("hs",),
                            ID_B, "id_bf")
                o, i = hTv[:, kc, j * 128:(j + 1) * 128], ptb[:, 0:128]
                P.op("vector", lambda e, o=o, i=i: e.tensor_copy(o, i),
                     reads=("ptb",), writes=("hsT",))
        for oc in range(8):
            items = [(wdv[:, kc, oc * 128:(oc + 1) * 128], hTv[:, kc, :],
                      pa[:, 0:512], ("wds", "hsT")) for kc in range(16)]
            P.matmuls("pa", items)
            o, i = yTv[:, oc, :], pa[:, 0:512]
            P.op("vector", lambda e, o=o, i=i: e.tensor_copy(o, i),
                 reads=("pa",), writes=("yT",))
        for j in range(4):
            for oc in range(8):
                P.transpose(ptb[:, 0:128], "ptb",
                            yTv[:, oc, j * 128:(j + 1) * 128], ("yT",),
                            ID_B, "id_bf")
                o, i = ywv[:, j, oc * 128:(oc + 1) * 128], ptb[:, 0:128]
                P.op("vector", lambda e, o=o, i=i: e.tensor_copy(o, i),
                     reads=("ptb",), writes=("ywst",))
            dst = yw_o[g * 512 + j * 128: g * 512 + (j + 1) * 128, :]
            P.dma("sync", dst, ywv[:, j, :], reads=("ywst",),
                  sem=f"yo{j % 4}")

    P.finish()
    return P.nc


def build_k5():
    P = Prog("k5")
    x1t = P.dram("x1tok", [S, D], BF16, kind="ExternalInput").ap()
    ywi = P.dram("YW", [NSLOT0, 1024], BF16, kind="ExternalInput").ap()
    slot_i = P.dram("slot2", [128, 2 * NT], F32, kind="ExternalInput").ap()
    gn_i = P.dram("gn2", [128, 2 * NT], F32, kind="ExternalInput").ap()
    x2_o = P.dram("x2tok", [S, D], BF16, kind="ExternalOutput").ap()

    sl = P.sbuf("sl", [128, 2 * NT], F32)
    gn = P.sbuf("gn", [128, 2 * NT], F32)
    sli = P.sbuf("sli", [128, 2 * NT], I32)
    xstg = P.sbuf("xstg", [128, 2 * 1024], BF16)
    gat = P.sbuf("gat", [128, 2 * 1024], BF16)
    acc = P.sbuf("acc", [128, 1024], F32)
    ob = P.sbuf("ob", [128, 2 * 1024], BF16)

    P.dma("sync", sl[:, :], slot_i, writes=("sl",), sem="d0")
    P.dma("sync", gn[:, :], gn_i, writes=("gn",), sem="d0")
    o = sl[:, :]
    P.op("vector", lambda e, o=o: e.tensor_scalar(
        o, o, float(NSLOT0 - 1), None, op0=ALU.min), reads=("sl",),
        writes=("sl",))
    o2, i2 = sli[:, :], sl[:, :]
    P.op("vector", lambda e, o=o2, i=i2: e.tensor_copy(o, i), reads=("sl",),
         writes=("sli",))

    sliv = sli[:, :].rearrange("p (k j) -> p k j", k=2)
    gnv = gn[:, :].rearrange("p (k j) -> p k j", k=2)
    x1v = x1t.rearrange("(j p) d -> j p d", p=128)
    x2v = x2_o.rearrange("(j p) d -> j p d", p=128)

    for j in range(NT):
        xsl = j % 2
        xdst = xstg[:, xsl * 1024:(xsl + 1) * 1024]
        P.dma("sync", xdst, x1v[j], writes=(f"xst{xsl}",), sem=f"dx{xsl}")
        o = acc[:, :]
        P.op("vector", lambda e, o=o, i=xdst: e.tensor_copy(o, i),
             reads=(f"xst{xsl}",), writes=("acc",))
        for k in range(2):
            slot_col = sliv[:, k, j:j + 1]
            gdst = gat[:, k * 1024:(k + 1) * 1024]
            gk = f"gt{k}"
            P.idma(gdst, None, ywi, IndirectOffsetOnAxis(slot_col, axis=0),
                   reads=("sli",), writes=(gk,), sem=f"dg{k}")
            wcol = gnv[:, k, j:j + 1]
            P.op("vector", lambda e, o=o, a=gdst, w=wcol:
                 e.scalar_tensor_tensor(o, a, w, o, op0=ALU.mult,
                                        op1=ALU.add),
                 reads=(gk, "gn", "acc"), writes=("acc",))
        os_ = ob[:, xsl * 1024:(xsl + 1) * 1024]
        P.op("vector", lambda e, o=os_, i=o: e.tensor_copy(o, i),
             reads=("acc",), writes=(f"ob{xsl}",))
        P.dma("sync", x2v[j], os_, reads=(f"ob{xsl}",), sem=f"do{xsl}")

    P.finish()
    return P.nc


def build_k6a():
    P = Prog("k6a")
    x2T = P.dram("x2T", [D, S], BF16, kind="ExternalInput").ap()
    wqkv = P.dram("w_qkv", [D, 3 * D], F32, kind="ExternalInput").ap()
    bqkv = P.dram("b_qkv", [1, 3 * D], F32, kind="ExternalInput").ap()
    wproj = P.dram("w_proj", [D, D], F32, kind="ExternalInput").ap()
    bproj = P.dram("b_proj", [1, D], F32, kind="ExternalInput").ap()
    x3T_o = P.dram("x3T", [D, S], BF16, kind="ExternalOutput").ap()

    xln = P.sbuf("xln", [128, ND * S], BF16)
    qs = P.sbuf("qs", [128, ND * S], BF16)
    kt = P.sbuf("kt", [128, NT * D], BF16)
    vt = P.sbuf("vt", [128, NT * D], BF16)
    wb = P.sbuf("wb", [128, 8 * 1024], BF16)
    br = P.sbuf("br", [1, 4 * 1024], BF16)
    kvs = P.sbuf("kvs", [128, 8 * 64], BF16)
    kss = P.sbuf("kss", [128, 8], BF16)
    etm = P.sbuf("etm", [128, S], F32)
    zr = P.sbuf("zr", [1, S], F32)
    zbc = P.sbuf("zbc", [128, S], F32)
    x2s = P.sbuf("x2s", [128, S], BF16)
    stat = P.sbuf("stat", [128, S], F32)
    mbc = P.sbuf("mbc", [128, S], BF16)
    rbc = P.sbuf("rbc", [128, S], BF16)
    sq = P.sbuf("sq", [128, 512], BF16)
    x3st = P.sbuf("x3st", [128, 2 * 512], BF16)

    pa = P.psum("pa", [128, 512], F32)
    pb = P.psum("pb", [128, 512], F32)
    pc = P.psum("pc", [128, 512], F32)
    pk = P.psum("pk", [128, 128], F32)

    consts = setup_consts(P)

    xv = xln[:, :].rearrange("p (c t) -> p c t", t=S)
    qv = qs[:, :].rearrange("p (c t) -> p c t", t=S)
    kv_ = kt[:, :].rearrange("p (t d) -> p t d", d=D)
    vv = vt[:, :].rearrange("p (t d) -> p t d", d=D)
    wbv = wb[:, :].rearrange("p (k o) -> p k o", o=1024)
    brv = br[0:1, :].rearrange("p (i o) -> p i o", o=1024)
    kvv = kvs[:, :].rearrange("p (h d) -> p h d", d=64)

    for ch in range(ND):
        P.dma("sync", xv[:, ch, :], x2T[ch * 128:(ch + 1) * 128, :],
              writes=("xln",), sem="dq_x")
    P.dma("gpsimd", brv[:, 0:3, :].rearrange("p a b -> p (a b)"), bqkv,
          writes=("br",), sem="w6")
    P.dma("gpsimd", brv[:, 3, :], bproj, writes=("br",), sem="w6")

    ln_inplace(P, xv, "xln", stat, sq, "sq", pa, pb, mbc, rbc, consts)

    for kc in range(ND):
        P.dma("gpsimd", wbv[:, kc, :], wqkv[kc * 128:(kc + 1) * 128, 0:1024],
              writes=("wb",), sem="w0")
    for oc in range(ND):
        for nb in range(4):
            cols = slice(nb * 512, (nb + 1) * 512)
            items = [(wbv[:, kc, oc * 128:(oc + 1) * 128],
                      xv[:, kc, cols], pa[:, 0:512], ("wb", "xln"))
                     for kc in range(ND)]
            items.append((brv[0:1, 0, oc * 128:(oc + 1) * 128],
                          consts["ones_row_b"][0:1, 0:512], pa[:, 0:512],
                          ("br", "ones_row_b")))
            P.matmuls("pa", items)
            o1 = etm[:, 0:512]
            P.op("vector", lambda e, o=o1, i=pa[:, 0:512]: e.tensor_scalar(
                o, i, 0.0, None, op0=ALU.min), reads=("pa",), writes=("etm",))
            P.op("scalar", lambda e, o=o1: e.activation(o, o, AF.Exp),
                 reads=("etm",), writes=("etm",))
            o2 = qv[:, oc, cols]
            P.op("vector", lambda e, o=o2, i=pa[:, 0:512]: e.tensor_scalar(
                o, i, 0.0, None, op0=ALU.max), reads=("pa",), writes=("qs",))
            P.op("vector", lambda e, o=o2, b=o1: e.tensor_tensor(
                o, o, b, op=ALU.add), reads=("qs", "etm"), writes=("qs",))

    for part, dst, dkey, bidx in ((1, kv_, "kt", 1), (2, vv, "vt", 2)):
        for kc in range(ND):
            P.dma("gpsimd", wbv[:, kc, :],
                  wqkv[kc * 128:(kc + 1) * 128,
                       part * 1024:(part + 1) * 1024],
                  writes=("wb",), sem=f"w{part}")
        for tc in range(NT):
            for nb in range(2):
                cols = slice(nb * 512, (nb + 1) * 512)
                items = [(xv[:, kc, tc * 128:(tc + 1) * 128],
                          wbv[:, kc, cols], pb[:, 0:512], ("wb", "xln"))
                         for kc in range(ND)]
                items.append((consts["ones_row_b"][0:1, 0:128],
                              brv[0:1, bidx, cols], pb[:, 0:512],
                              ("br", "ones_row_b")))
                P.matmuls("pb", items)
                if part == 1:
                    o1 = etm[:, 0:512]
                    P.op("vector", lambda e, o=o1, i=pb[:, 0:512]:
                         e.tensor_scalar(o, i, 0.0, None, op0=ALU.min),
                         reads=("pb",), writes=("etm",))
                    P.op("scalar", lambda e, o=o1: e.activation(
                        o, o, AF.Exp), reads=("etm",), writes=("etm",))
                    o2 = dst[:, tc, cols]
                    P.op("vector", lambda e, o=o2, i=pb[:, 0:512]:
                         e.tensor_scalar(o, i, 0.0, None, op0=ALU.max),
                         reads=("pb",), writes=(dkey,))
                    P.op("vector", lambda e, o=o2, b=o1: e.tensor_tensor(
                        o, o, b, op=ALU.add), reads=(dkey, "etm"),
                        writes=(dkey,))
                else:
                    o2, i2 = dst[:, tc, cols], pb[:, 0:512]
                    P.op("vector", lambda e, o=o2, i=i2: e.tensor_copy(o, i),
                         reads=("pb",), writes=(dkey,))

    for h in range(16):
        hb = (h % 2) * 64
        hcols = slice(h * 64, (h + 1) * 64)
        items = [(kv_[:, tc, hcols], vv[:, tc, hcols],
                  pk[hb:hb + 64, 0:64], ("kt", "vt")) for tc in range(NT)]
        P.matmuls("pk", items)
        o, i = kvv[hb:hb + 64, h // 2, :], pk[hb:hb + 64, 0:64]
        P.op("vector", lambda e, o=o, i=i: e.tensor_copy(o, i),
             reads=("pk",), writes=("kvs",))
        items = [(kv_[:, tc, hcols], consts["ones_col_b"][:, 0:1],
                  pc[hb:hb + 64, 0:1], ("kt", "ones_col_b"))
                 for tc in range(NT)]
        P.matmuls("pc", items)
        o, i = kss[hb:hb + 64, h // 2:h // 2 + 1], pc[hb:hb + 64, 0:1]
        P.op("vector", lambda e, o=o, i=i: e.tensor_copy(o, i),
             reads=("pc",), writes=("kss",))

    cx = vt[:, :].rearrange("p (c t) -> p c t", t=S)
    for h in range(16):
        hb = (h % 2) * 64
        for nb in range(4):
            cols = slice(nb * 512, (nb + 1) * 512)
            items = [(kss[hb:hb + 64, h // 2:h // 2 + 1],
                      qv[hb:hb + 64, h // 2, cols], pc[0:1, 0:512],
                      ("kss", "qs"))]
            P.matmuls("pc", items)
            o, i = zr[0:1, cols], pc[0:1, 0:512]
            P.op("vector", lambda e, o=o, i=i: e.tensor_copy(o, i),
                 reads=("pc",), writes=("zr",))
        o = zr[0:1, 0:S]
        P.op("vector", lambda e, o=o: e.tensor_scalar(
            o, o, 1e-8, None, op0=ALU.add), reads=("zr",), writes=("zr",))
        P.op("vector", lambda e, o=o: e.reciprocal(o, o), reads=("zr",),
             writes=("zr",))
        for nb in range(4):
            cols = slice(nb * 512, (nb + 1) * 512)
            bcast_row(P, pc[hb:hb + 64, 0:512], "pc", zr[0:1, cols],
                      ("zr",), consts, 64)
            o, i = zbc[hb:hb + 64, cols], pc[hb:hb + 64, 0:512]
            P.op("vector", lambda e, o=o, i=i: e.tensor_copy(o, i),
                 reads=("pc",), writes=("zbc",))
            items = [(kvv[hb:hb + 64, h // 2, :],
                      qv[hb:hb + 64, h // 2, cols], pa[hb:hb + 64, 0:512],
                      ("kvs", "qs"))]
            P.matmuls("pa", items)
            o = cx[hb:hb + 64, h // 2, cols]
            a_, b_ = pa[hb:hb + 64, 0:512], zbc[hb:hb + 64, cols]
            P.op("vector", lambda e, o=o, a=a_, b=b_: e.tensor_tensor(
                o, a, b, op=ALU.mult), reads=("pa", "zbc"), writes=("vt",))

    for kc in range(ND):
        P.dma("gpsimd", wbv[:, kc, :], wproj[kc * 128:(kc + 1) * 128, :],
              writes=("wb",), sem="w3")
    for oc in range(ND):
        P.dma("sync", x2s[:, :], x2T[oc * 128:(oc + 1) * 128, :],
              writes=("x2s",), sem=f"x2r{oc % 2}")
        for nb in range(4):
            cols = slice(nb * 512, (nb + 1) * 512)
            items = [(wbv[:, kc, oc * 128:(oc + 1) * 128],
                      cx[:, kc, cols], pa[:, 0:512], ("wb", "vt"))
                     for kc in range(ND)]
            items.append((brv[0:1, 3, oc * 128:(oc + 1) * 128],
                          consts["ones_row_b"][0:1, 0:512], pa[:, 0:512],
                          ("br", "ones_row_b")))
            P.matmuls("pa", items)
            st = x3st[:, (nb % 2) * 512:(nb % 2 + 1) * 512]
            P.op("vector", lambda e, o=st, a=pa[:, 0:512],
                 b=x2s[:, cols]: e.tensor_tensor(o, a, b, op=ALU.add),
                 reads=("pa", "x2s"), writes=(f"x3st{nb % 2}",))
            P.dma("sync", x3T_o[oc * 128:(oc + 1) * 128, cols], st,
                  reads=(f"x3st{nb % 2}",), sem=f"xo{nb % 2}")

    P.finish()
    return P.nc


E1, K1_, CAP1 = 64, 6, 384


def build_k6b():
    P = Prog("k6b")
    x3T = P.dram("x3T", [D, S], BF16, kind="ExternalInput").ap()
    w64 = P.dram("w64T", [D, E1], F32, kind="ExternalInput").ap()
    b64 = P.dram("b64", [1, E1], F32, kind="ExternalInput").ap()
    sw1 = P.dram("sw1T", [D, 2048], F32, kind="ExternalInput").ap()
    sw2 = P.dram("sw2T", [2048, D], F32, kind="ExternalInput").ap()
    par_o = P.dram("partialT", [D, S], BF16, kind="ExternalOutput").ap()
    xln_o = P.dram("xln4T", [D, S], BF16, kind="ExternalOutput").ap()
    idx_o = P.dram("idx6", [128, K1_ * NT], F32, kind="ExternalOutput").ap()
    pr_o = P.dram("pr6", [128, K1_ * NT], F32, kind="ExternalOutput").ap()
    pos_o = P.dram("pos6", [128, K1_ * NT], F32, kind="ExternalOutput").ap()
    cnt_o = P.dram("cnt64", [E1, 1], F32, kind="ExternalOutput").ap()

    xs = P.sbuf("xs", [128, ND * S], BF16)
    xln = P.sbuf("xln", [128, ND * S], BF16)
    z1 = P.sbuf("z1", [128, 16 * S], BF16)
    wb = P.sbuf("wb", [128, 8 * 1024], BF16)
    gsc = P.sbuf("gsc", [128, S], BF16)
    stat = P.sbuf("stat", [128, S], F32)
    mbc = P.sbuf("mbc", [128, S], BF16)
    rbc = P.sbuf("rbc", [128, S], BF16)
    sq = P.sbuf("sq", [128, 512], BF16)
    lgt = P.sbuf("lgt", [E1, S], BF16)
    ltok = P.sbuf("ltok", [128, NT * E1], F32)
    sc = P.sbuf("sc", [128, NT * E1], F32)
    sel = ltok
    t3a = P.sbuf("t3a", [128, NT * E1], F32)
    io64 = P.sbuf("io64", [128, NT * E1], BF16)
    mtok = P.sbuf("mtok", [128, NT * E1], BF16)
    cexT = P.sbuf("cexT", [E1, S], F32)
    ctok = P.sbuf("ctok", [128, NT * E1], F32)
    rt = P.sbuf("rt", [128, 64], F32)
    bbc = P.sbuf("bbc", [128, E1], F32)
    idx6 = P.sbuf("idx6s", [128, K1_ * NT], F32)
    pr6 = P.sbuf("pr6s", [128, K1_ * NT], F32)
    pos6 = P.sbuf("pos6s", [128, K1_ * NT], F32)
    w64b = P.sbuf("w64b", [128, 8 * E1], BF16)

    pa = P.psum("pa", [128, 512], F32)
    pb = P.psum("pb", [128, 512], F32)
    pc = P.psum("pc", [128, 512], F32)
    pt = P.psum("pt", [128, 512], F32)
    ptb = P.psum("ptb", [128, 128], BF16)

    consts = setup_consts(P)
    ID_B = consts["id_bf"][:, :]
    ID_B64 = consts["id_bf"][0:64, 0:64]

    xv = xs[:, :].rearrange("p (c t) -> p c t", t=S)
    xlv = xln[:, :].rearrange("p (c t) -> p c t", t=S)
    z1v = z1[:, :].rearrange("p (c t) -> p c t", t=S)
    wbv = wb[:, :].rearrange("p (k o) -> p k o", o=1024)

    for ch in range(ND):
        P.dma("sync", xv[:, ch, :], x3T[ch * 128:(ch + 1) * 128, :],
              writes=("xs",), sem="dq_x")
    for ch in range(ND):
        o, i = xlv[:, ch, :], xv[:, ch, :]
        P.op("vector", lambda e, o=o, i=i: e.tensor_copy(o, i),
             reads=("xs",), writes=("xln",))
    ln_inplace(P, xlv, "xln", stat, sq, "sq", pa, pb, mbc, rbc, consts)
    for ch in range(ND):
        P.dma("sync", xln_o[ch * 128:(ch + 1) * 128, :], xlv[:, ch, :],
              reads=("xln",), sem="dq_out")

    w64v = w64b[:, :].rearrange("p (k o) -> p k o", o=E1)
    for kc in range(ND):
        P.dma("gpsimd", w64v[:, kc, :], w64[kc * 128:(kc + 1) * 128, :],
              writes=("w64b",), sem="w5")
    for nb in range(4):
        cols = slice(nb * 512, (nb + 1) * 512)
        items = [(w64v[:, kc, :], xlv[:, kc, cols], pc[0:E1, 0:512],
                  ("w64b", "xln")) for kc in range(ND)]
        P.matmuls("pc", items)
        o, i = lgt[:, cols], pc[0:E1, 0:512]
        P.op("vector", lambda e, o=o, i=i: e.tensor_copy(o, i),
             reads=("pc",), writes=("lgt",))
    l3 = ltok[:, :].rearrange("p (j e) -> p j e", e=E1)
    for j in range(NT):
        P.transpose(ptb[:, 0:E1], "ptb", lgt[0:E1, j * 128:(j + 1) * 128],
                    ("lgt",), ID_B64, "id_bf")
        o, i = l3[:, j, :], ptb[:, 0:E1]
        P.op("vector", lambda e, o=o, i=i: e.tensor_copy(o, i),
             reads=("ptb",), writes=("ltok",))
    o, i = sc[:, :], ltok[:, :]
    P.op("scalar", lambda e, o=o, i=i: e.activation(o, i, AF.Sigmoid),
         reads=("ltok",), writes=("sc",))
    P.dma("sync", bbc[0:1, :], b64, writes=("bbc0",), sem="d0")
    bcast_row(P, pc[:, 0:E1], "pc", bbc[0:1, :], ("bbc0",), consts, 128)
    o, i = bbc[:, :], pc[:, 0:E1]
    P.op("vector", lambda e, o=o, i=i: e.tensor_copy(o, i), reads=("pc",),
         writes=("bbc",))
    s3 = sel[:, :].rearrange("p (j e) -> p j e", e=E1)
    sc3 = sc[:, :].rearrange("p (j e) -> p j e", e=E1)
    bb3 = bbc[:, :].rearrange("p (u e) -> p u e", u=1) \
        .broadcast_to([128, NT, E1])
    P.op("vector", lambda e, o=s3, a=sc3, b=bb3: e.tensor_tensor(
        o, a, b, op=ALU.add), reads=("sc", "bbc", "ltok"), writes=("ltok",))

    a = io64[:, :]
    P.op("gpsimd", lambda e, a=a: e.iota(a, pattern=[[0, NT], [-1, E1]],
         base=64, channel_multiplier=0,
         allow_small_or_imprecise_dtypes=True), writes=("io64",))
    a = mtok[:, :]
    P.op("gpsimd", lambda e, a=a: e.memset(a, 0.0), writes=("mtok",))

    io3 = io64[:, :].rearrange("p (j e) -> p j e", e=E1)
    m3 = mtok[:, :].rearrange("p (j e) -> p j e", e=E1)
    t3 = t3a[:, :].rearrange("p (j e) -> p j e", e=E1)
    mrow = rt[:, 0:NT]
    mrow3 = mrow.rearrange("p (j u) -> p j u", u=1).broadcast_to([128, NT, E1])
    iv = idx6[:, :].rearrange("p (k j) -> p k j", k=K1_)
    pv = pr6[:, :].rearrange("p (k j) -> p k j", k=K1_)
    ssum = rt[:, 32:32 + NT]

    idm = rt[:, 16:32]
    idm3 = idm.rearrange("p (j u) -> p j u", u=1).broadcast_to([128, NT, E1])
    for k in range(K1_):
        P.op("vector", lambda e, o=mrow, i=s3: e.reduce_max(o, i, axis=AX.X),
             reads=("ltok",), writes=("rt",))
        P.op("vector", lambda e, o=t3, a=s3, b=mrow3: e.tensor_tensor(
            o, a, b, op=ALU.is_equal), reads=("ltok", "rt"), writes=("t3a",))
        P.op("vector", lambda e, o=t3, b=io3: e.tensor_tensor(
            o, o, b, op=ALU.mult), reads=("t3a", "io64"), writes=("t3a",))
        P.op("vector", lambda e, o=idm, i=t3: e.tensor_reduce(
            o, i, axis=AX.X, op=ALU.max), reads=("t3a",), writes=("rt",))
        o = iv[:, k, :]
        P.op("vector", lambda e, o=o, i=idm: e.tensor_scalar(
            o, i, -1.0, 64.0, op0=ALU.mult, op1=ALU.add),
            reads=("rt",), writes=("idx6s",))
        P.op("vector", lambda e, o=t3, a=io3, b=idm3: e.tensor_tensor(
            o, a, b, op=ALU.is_equal), reads=("io64", "rt"),
            writes=("t3a",))
        P.op("vector", lambda e, o=m3, b=t3: e.tensor_tensor(
            o, o, b, op=ALU.add), reads=("mtok", "t3a"), writes=("mtok",))
        P.op("vector", lambda e, o=s3, a=t3: e.scalar_tensor_tensor(
            o, a, -1024.0, o, op0=ALU.mult, op1=ALU.add),
            reads=("t3a", "ltok"), writes=("ltok",))
        P.op("vector", lambda e, o=t3, b=sc3: e.tensor_tensor(
            o, o, b, op=ALU.mult), reads=("t3a", "sc"), writes=("t3a",))
        o2 = pv[:, k, :]
        P.op("vector", lambda e, o=o2, i=t3: e.reduce_sum(o, i, axis=AX.X),
             reads=("t3a",), writes=("pr6s",))

    o = ssum
    P.op("vector", lambda e, o=o, i=pv: e.tensor_copy(o, i[:, 0, :]),
         reads=("pr6s",), writes=("rt",))
    for k in range(1, K1_):
        P.op("vector", lambda e, o=o, a=pv, k=k: e.tensor_tensor(
            o, o, a[:, k, :], op=ALU.add), reads=("rt", "pr6s"),
            writes=("rt",))
    P.op("vector", lambda e, o=o: e.tensor_scalar(
        o, o, 1e-20, None, op0=ALU.add), reads=("rt",), writes=("rt",))
    P.op("vector", lambda e, o=o: e.reciprocal(o, o), reads=("rt",),
         writes=("rt",))
    for k in range(K1_):
        ok_ = pv[:, k, :]
        P.op("vector", lambda e, o=ok_, b=ssum: e.tensor_tensor(
            o, o, b, op=ALU.mult), reads=("pr6s", "rt"), writes=("pr6s",))

    maskT = lgt
    for j in range(NT):
        P.transpose(ptb[0:E1, 0:128], "ptb", m3[:, j, :], ("mtok",),
                    ID_B, "id_bf")
        o, i = maskT[0:E1, j * 128:(j + 1) * 128], ptb[0:E1, 0:128]
        P.op("vector", lambda e, o=o, i=i: e.tensor_copy(o, i),
             reads=("ptb",), writes=("lgt",))
    a, b_ = cexT[0:E1, :], maskT[0:E1, :]
    P.op("vector", lambda e, o=a, m=b_: e.tensor_tensor_scan(
        o, m, m, 0.0, op0=ALU.add, op1=ALU.bypass),
        reads=("lgt",), writes=("cexT",))
    o, i = rt[0:E1, 48:49], cexT[0:E1, S - 1:S]
    P.op("vector", lambda e, o=o, i=i: e.tensor_copy(o, i),
         reads=("cexT",), writes=("rt",))
    P.dma("sync", cnt_o[:, :], rt[0:E1, 48:49], reads=("rt",), sem="dcnt")
    P.op("vector", lambda e, o=a, m=b_: e.tensor_tensor(
        o, o, m, op=ALU.subtract), reads=("cexT", "lgt"), writes=("cexT",))
    c3 = ctok[:, :].rearrange("p (j e) -> p j e", e=E1)
    for j in range(NT):
        P.transpose(pt[:, 0:E1], "pt", cexT[0:E1, j * 128:(j + 1) * 128],
                    ("cexT",), consts["id_f32"][0:E1, 0:E1], "id_f32")
        o, i = c3[:, j, :], pt[:, 0:E1]
        P.op("vector", lambda e, o=o, i=i: e.tensor_copy(o, i),
             reads=("pt",), writes=("ctok",))
    p6 = pos6[:, :].rearrange("p (k j) -> p k j", k=K1_)
    for k in range(K1_):
        P.op("vector", lambda e, o=idm, i=iv, k=k: e.tensor_scalar(
            o, i[:, k, :], -1.0, 64.0, op0=ALU.mult, op1=ALU.add),
            reads=("idx6s",), writes=("rt",))
        P.op("vector", lambda e, o=t3, a=io3, b=idm3: e.tensor_tensor(
            o, a, b, op=ALU.is_equal), reads=("io64", "rt"),
            writes=("t3a",))
        P.op("vector", lambda e, o=t3, b=c3: e.tensor_tensor(
            o, o, b, op=ALU.mult), reads=("t3a", "ctok"), writes=("t3a",))
        o = p6[:, k, :]
        P.op("vector", lambda e, o=o, i=t3: e.reduce_sum(o, i, axis=AX.X),
             reads=("t3a",), writes=("pos6s",))

    P.dma("sync", idx_o, idx6[:, :], reads=("idx6s",), sem="dq_out")
    P.dma("sync", pr_o, pr6[:, :], reads=("pr6s",), sem="dq_out")
    P.dma("sync", pos_o, pos6[:, :], reads=("pos6s",), sem="dq_out")

    for half in range(2):
        for kc in range(ND):
            o = wbv[:, kc, :]
            i = sw1[kc * 128:(kc + 1) * 128, half * 1024:(half + 1) * 1024]
            P.dma("gpsimd", o, i, writes=("wb",), sem=f"w{half}")
        for oc in range(8):
            row = half * 8 + oc
            for nb in range(4):
                cols = slice(nb * 512, (nb + 1) * 512)
                items = [(wbv[:, kc, oc * 128:(oc + 1) * 128],
                          xlv[:, kc, cols], pa[:, 0:512], ("wb", "xln"))
                         for kc in range(ND)]
                P.matmuls("pa", items)
                o, i = z1v[:, row, cols], pa[:, 0:512]
                P.op("vector", lambda e, o=o, i=i: e.tensor_copy(o, i),
                     reads=("pa",), writes=("z1",))
            tb = z1v[:, row, :]
            sqr = gsc[:, 0:S]
            P.op("scalar", lambda e, o=sqr, i=tb: e.activation(
                o, i, AF.Square), reads=("z1",), writes=("gsc",))
            P.op("vector", lambda e, o=sqr, b=tb: e.tensor_tensor(
                o, o, b, op=ALU.mult), reads=("gsc", "z1"), writes=("gsc",))
            P.op("vector", lambda e, o=sqr, b=tb: e.scalar_tensor_tensor(
                o, o, 0.044715, b, op0=ALU.mult, op1=ALU.add),
                reads=("gsc", "z1"), writes=("gsc",))
            P.op("scalar", lambda e, o=sqr: e.activation(
                o, o, AF.Tanh, bias=0.0, scale=0.7978845608028654),
                reads=("gsc",), writes=("gsc",))
            P.op("vector", lambda e, o=sqr: e.tensor_scalar(
                o, o, 1.0, None, op0=ALU.add), reads=("gsc",),
                writes=("gsc",))
            P.op("vector", lambda e, o=tb, b=sqr: e.tensor_tensor(
                o, o, b, op=ALU.mult), reads=("z1", "gsc"), writes=("z1",))
            P.op("vector", lambda e, o=tb: e.tensor_scalar(
                o, o, 0.5, None, op0=ALU.mult), reads=("z1",), writes=("z1",))

    wb2 = wb[:, :].rearrange("p (k o) -> p k o", o=512)
    for ohalf in range(2):
        for kc in range(16):
            o = wb2[:, kc, :]
            i = sw2[kc * 128:(kc + 1) * 128,
                    ohalf * 512:(ohalf + 1) * 512]
            P.dma("gpsimd", o, i, writes=("wb",), sem=f"w{2 + ohalf}")
        for oc4 in range(4):
            oc = ohalf * 4 + oc4
            for nb in range(4):
                cols = slice(nb * 512, (nb + 1) * 512)
                items = [(wb2[:, kc, oc4 * 128:(oc4 + 1) * 128],
                          z1v[:, kc, cols], pb[:, 0:512],
                          ("wb", "z1")) for kc in range(16)]
                P.matmuls("pb", items)
                o = xv[:, oc, cols]
                P.op("vector", lambda e, o=o, a=pb[:, 0:512]:
                     e.scalar_tensor_tensor(o, a, 1.0, o, op0=ALU.mult,
                                            op1=ALU.add),
                     reads=("pb", "xs"), writes=("xs",))

    for ch in range(ND):
        P.dma("sync", par_o[ch * 128:(ch + 1) * 128, :], xv[:, ch, :],
              reads=("xs",), sem="dq_out")
    P.finish()
    return P.nc


NSLOT1 = E1 * CAP1
NREC1 = NSLOT1 + 128


def build_k7():
    P = Prog("k7")
    idx_i = P.dram("idx6", [128, K1_ * NT], F32, kind="ExternalInput").ap()
    pr_i = P.dram("pr6", [128, K1_ * NT], F32, kind="ExternalInput").ap()
    pos_i = P.dram("pos6", [128, K1_ * NT], F32, kind="ExternalInput").ap()
    cnts_i = P.dram("cnts64", [8, E1], F32, kind="ExternalInput").ap()
    rec_o = P.dram("rec1", [NREC1, 1], F32, kind="ExternalOutput").ap()
    slot_o = P.dram("slot6", [128, K1_ * NT], F32, kind="ExternalOutput").ap()
    pw_o = P.dram("pw6", [128, K1_ * NT], F32, kind="ExternalOutput").ap()
    pid_t = P.nc.partition_id_tensor.ap()

    sio = P.sbuf("sio", [128, K1_ * NT], F32)
    spr = P.sbuf("spr", [128, K1_ * NT], F32)
    spo = P.sbuf("spo", [128, K1_ * NT], F32)
    scn = P.sbuf("scn", [8, E1], F32)
    spid = P.sbuf("spid", [1, 1], mybir.dt.uint32)
    spidf = P.sbuf("spidf", [1, 1], F32)
    smk = P.sbuf("smk", [8, 1], F32)
    sprefix = P.sbuf("sprefix", [1, E1], F32)
    sprefb = P.sbuf("sprefb", [128, E1], F32)
    spidc = P.sbuf("spidc", [128, 1], F32)
    io64 = P.sbuf("io64", [128, NT * E1], BF16)
    iot = P.sbuf("iot", [128, NT], F32)
    t3a = P.sbuf("t3a", [128, NT * E1], F32)
    gp = P.sbuf("gp", [128, K1_ * NT], F32)
    kp = P.sbuf("kp", [128, K1_ * NT], F32)
    sl = P.sbuf("sl", [128, K1_ * NT], F32)
    tk = P.sbuf("tk", [128, NT], F32)
    sli = P.sbuf("sli", [128, K1_ * NT], I32)
    sent = P.sbuf("sent", [128, K1_ * NT], F32)
    zero = P.sbuf("zero", [128, 512], F32)

    pa = P.psum("pa", [128, 512], F32)
    pb = P.psum("pb", [128, 512], F32)

    consts = setup_consts(P)

    P.dma("sync", sio[:, :], idx_i, writes=("sio",), sem="d0")
    P.dma("sync", spr[:, :], pr_i, writes=("spr",), sem="d0")
    P.dma("sync", spo[:, :], pos_i, writes=("spo",), sem="d0")
    P.dma("sync", scn[:, :], cnts_i, writes=("scn",), sem="d0")
    P.dma("sync", spid[:, :], pid_t, writes=("spid",), sem="d0")

    a = zero[:, :]
    P.op("gpsimd", lambda e, a=a: e.memset(a, 0.0), writes=("zero",))
    recv = rec_o[0:NREC1, :].rearrange("(p r) c -> p (r c)", p=128)
    P.dma("gpsimd", recv, zero[:, 0:NREC1 // 128], reads=("zero",),
          writes=("rec1",), sem="dz")
    a = sent[:, :]
    P.op("gpsimd", lambda e, a=a: e.iota(a, pattern=[[1, K1_ * NT]],
         base=NSLOT1, channel_multiplier=0,
         allow_small_or_imprecise_dtypes=True), writes=("sent",))

    o, i = spidf[:, :], spid[:, :]
    P.op("vector", lambda e, o=o, i=i: e.tensor_copy(o, i),
         reads=("spid",), writes=("spidf",))
    bcast_row(P, pa[0:8, 0:1], "pa", spidf[0:1, 0:1], ("spidf",), consts, 8)
    o, i = smk[:, :], pa[0:8, 0:1]
    P.op("vector", lambda e, o=o, i=i: e.tensor_copy(o, i), reads=("pa",),
         writes=("smk",))
    a, b_ = smk[:, :], consts["iota_col"][0:8, 0:1]
    P.op("vector", lambda e, o=a, b=b_: e.tensor_tensor(
        o, b, o, op=ALU.is_lt), reads=("smk", "iota_col"), writes=("smk",))
    P.matmuls("pb", [(smk[:, :], scn[:, :], pb[0:1, 0:E1], ("smk", "scn"))])
    o, i = sprefix[:, :], pb[0:1, 0:E1]
    P.op("vector", lambda e, o=o, i=i: e.tensor_copy(o, i), reads=("pb",),
         writes=("sprefix",))
    bcast_row(P, pa[:, 0:E1], "pa", sprefix[0:1, :], ("sprefix",), consts,
              128)
    o, i = sprefb[:, :], pa[:, 0:E1]
    P.op("vector", lambda e, o=o, i=i: e.tensor_copy(o, i), reads=("pa",),
         writes=("sprefb",))
    bcast_row(P, pb[:, 0:1], "pb", spidf[0:1, 0:1], ("spidf",), consts, 128)
    o, i = spidc[:, :], pb[:, 0:1]
    P.op("vector", lambda e, o=o, i=i: e.tensor_copy(o, i), reads=("pb",),
         writes=("spidc",))

    a = io64[:, :]
    P.op("gpsimd", lambda e, a=a: e.iota(a, pattern=[[0, NT], [1, E1]],
         base=0, channel_multiplier=0, allow_small_or_imprecise_dtypes=True),
         writes=("io64",))
    a = iot[:, :]
    P.op("gpsimd", lambda e, a=a: e.iota(a, pattern=[[128, NT]], base=0,
         channel_multiplier=1, allow_small_or_imprecise_dtypes=True),
         writes=("iot",))
    o = spidc[:, 0:1]
    P.op("vector", lambda e, o=o: e.tensor_scalar(
        o, o, 2048.0, None, op0=ALU.mult), reads=("spidc",),
        writes=("spidc",))
    o, a, b_ = tk[:, :], iot[:, :], spidc[:, 0:1]
    P.op("vector", lambda e, o=o, a=a, b=b_: e.tensor_scalar(
        o, a, b, None, op0=ALU.add), reads=("iot", "spidc"), writes=("tk",))

    iov = io64[:, :].rearrange("p (j e) -> p j e", e=E1)
    t3 = t3a[:, :].rearrange("p (j e) -> p j e", e=E1)
    prefb3 = sprefb[:, :].rearrange("p (u e) -> p u e", u=1) \
        .broadcast_to([128, NT, E1])
    iv = sio[:, :].rearrange("p (k j) -> p k j", k=K1_)
    prv = spr[:, :].rearrange("p (k j) -> p k j", k=K1_)
    pv = spo[:, :].rearrange("p (k j) -> p k j", k=K1_)
    gpv = gp[:, :].rearrange("p (k j) -> p k j", k=K1_)
    kpv = kp[:, :].rearrange("p (k j) -> p k j", k=K1_)
    slv = sl[:, :].rearrange("p (k j) -> p k j", k=K1_)
    sev = sent[:, :].rearrange("p (k j) -> p k j", k=K1_)

    for k in range(K1_):
        ib3 = iv[:, k, :].rearrange("p (j u) -> p j u", u=1) \
            .broadcast_to([128, NT, E1])
        P.op("vector", lambda e, o=t3, a=iov, b=ib3: e.tensor_tensor(
            o, a, b, op=ALU.is_equal), reads=("io64", "sio"),
            writes=("t3a",))
        P.op("vector", lambda e, o=t3, b=prefb3: e.tensor_tensor(
            o, o, b, op=ALU.mult), reads=("t3a", "sprefb"), writes=("t3a",))
        o = gpv[:, k, :]
        P.op("vector", lambda e, o=o, i=t3: e.reduce_sum(o, i, axis=AX.X),
             reads=("t3a",), writes=("gp",))
        a = pv[:, k, :]
        P.op("vector", lambda e, o=o, a=a: e.tensor_tensor(
            o, o, a, op=ALU.add), reads=("gp", "spo"), writes=("gp",))
        ok_ = kpv[:, k, :]
        P.op("vector", lambda e, o=ok_, a=o: e.tensor_scalar(
            o, a, float(CAP1), None, op0=ALU.is_lt), reads=("gp",),
            writes=("kp",))
        sv = sev[:, k, :]
        os_ = slv[:, k, :]
        a = iv[:, k, :]
        P.op("vector", lambda e, o=os_, a=a: e.tensor_scalar(
            o, a, float(CAP1), None, op0=ALU.mult), reads=("sio",),
            writes=("sl",))
        P.op("vector", lambda e, o=os_, b=o: e.tensor_tensor(
            o, o, b, op=ALU.add), reads=("sl", "gp"), writes=("sl",))
        P.op("vector", lambda e, o=os_, b=sv: e.tensor_tensor(
            o, o, b, op=ALU.subtract), reads=("sl", "sent"), writes=("sl",))
        P.op("vector", lambda e, o=os_, b=ok_: e.tensor_tensor(
            o, o, b, op=ALU.mult), reads=("sl", "kp"), writes=("sl",))
        P.op("vector", lambda e, o=os_, b=sv: e.tensor_tensor(
            o, o, b, op=ALU.add), reads=("sl", "sent"), writes=("sl",))
        og = prv[:, k, :]
        P.op("vector", lambda e, o=og, b=ok_: e.tensor_tensor(
            o, o, b, op=ALU.mult), reads=("spr", "kp"), writes=("spr",))

    o, i = sli[:, :], sl[:, :]
    P.op("vector", lambda e, o=o, i=i: e.tensor_copy(o, i), reads=("sl",),
         writes=("sli",))
    sliv = sli[:, :].rearrange("p (k j) -> p k j", k=K1_)
    for k in range(K1_):
        for j in range(NT):
            src = tk[:, j:j + 1]
            off = sliv[:, k, j:j + 1]
            P.idma(rec_o, IndirectOffsetOnAxis(off, axis=0), src, None,
                   reads=("tk", "sli", "rec1"), sem=f"ix{(k * NT + j) % 4}")

    P.dma("sync", slot_o, sl[:, :], reads=("sl",), sem="dq_out")
    P.dma("sync", pw_o, spr[:, :], reads=("spr",), sem="dq_out")
    P.finish()
    return P.nc


def build_k8():
    P = Prog("k8")
    recs = P.dram("recs", [8, 8 * CAP1, 1], F32, kind="ExternalInput").ap()
    xtok = P.dram("xtok", [16384, D], BF16, kind="ExternalInput").ap()
    w1s = P.dram("w1s", [8, D, 512], F32, kind="ExternalInput").ap()
    w2s = P.dram("w2s", [8, 512, D], F32, kind="ExternalInput").ap()
    yw_o = P.dram("yw1", [8 * CAP1, D], BF16, kind="ExternalOutput").ap()
    recs_hbm = P.dram("recs_hbm", [8 * CAP1, 1], F32, kind="Internal").ap()

    NSL = 8 * CAP1
    NCH = NSL // 128
    w1b = P.sbuf("w1b", [128, 8 * 512], BF16)
    w2b = P.sbuf("w2b", [128, 4 * 1024], BF16)
    rsum = P.sbuf("rsum", [128, 8 * (NSL // 128)], F32)
    rtmp = P.sbuf("rtmp", [128, NSL // 128], F32)
    rint = P.sbuf("rint", [128, NSL // 128], I32)
    rcol = P.sbuf("rcol", [128, NSL // 128], F32)
    xg = P.sbuf("xg", [128, D], BF16)
    xT = P.sbuf("xT", [128, 8 * 384], BF16)
    hT = P.sbuf("hT", [128, 4 * 384], BF16)
    gsc = P.sbuf("gsc", [128, 4 * 384], BF16)
    yT = P.sbuf("yT", [128, 8 * 384], BF16)
    ywst = P.sbuf("ywst", [128, 3 * 1024], BF16)

    pa = P.psum("pa", [128, 4 * 512], F32)
    ptb = P.psum("ptb", [128, 512], BF16)

    consts = setup_consts(P)
    ID_B = consts["id_bf"][:, :]

    rv = rsum[:, :].rearrange("p (c r) -> p c r", r=NCH)
    for c in range(8):
        vsl = recs[c, :, :].rearrange("(p r) u -> p (r u)", p=128)
        P.dma("sync", rv[:, c, :], vsl, writes=("rsum",), sem="d0")
    o = rtmp[:, :]
    P.op("vector", lambda e, o=o, a=rv: e.tensor_copy(o, a[:, 0, :]),
         reads=("rsum",), writes=("rtmp",))
    for c in range(1, 8):
        P.op("vector", lambda e, o=o, a=rv, c=c: e.tensor_tensor(
            o, o, a[:, c, :], op=ALU.add), reads=("rtmp", "rsum"),
            writes=("rtmp",))
    rb = recs_hbm.rearrange("(p r) u -> p (r u)", p=128)
    P.dma("sync", rb, rtmp[:, :], reads=("rtmp",), writes=("recs_hbm",),
          sem="d1")
    for r in range(NCH):
        P.dma("sync", rcol[:, r:r + 1],
              recs_hbm[r * 128:(r + 1) * 128, :],
              reads=("recs_hbm",), writes=("rcol",), sem="d2")
    o, i = rint[:, :], rcol[:, :]
    P.op("vector", lambda e, o=o, i=i: e.tensor_copy(o, i), reads=("rcol",),
         writes=("rint",))

    w1v = w1b[:, :].rearrange("p (k o) -> p k o", o=512)
    w2v = w2b[:, :].rearrange("p (k o) -> p k o", o=1024)
    xTv = xT[:, :].rearrange("p (k s) -> p k s", s=384)
    hTv = hT[:, :].rearrange("p (k s) -> p k s", s=384)
    yTv = yT[:, :].rearrange("p (o s) -> p o s", s=384)
    ywv = ywst[:, :].rearrange("p (j o) -> p j o", o=D)
    pav = pa[:, :].rearrange("p (f s) -> p f s", s=512)

    for le in range(8):
        for kc in range(8):
            P.dma("gpsimd", w1v[:, kc, :],
                  w1s[le, kc * 128:(kc + 1) * 128, :], writes=("w1b",),
                  sem=f"w{le % 2}")
        for fc in range(4):
            P.dma("gpsimd", w2v[:, fc, :],
                  w2s[le, fc * 128:(fc + 1) * 128, :], writes=("w2b",),
                  sem=f"w{le % 2}")
        for j in range(3):
            r = le * 3 + j
            off = rint[:, r:r + 1]
            P.idma(xg[:, :], None, xtok, IndirectOffsetOnAxis(off, axis=0),
                   reads=("rint",), writes=("xg",), sem="ixg")
            for kc in range(8):
                P.transpose(ptb[:, 0:128], "ptb",
                            xg[:, kc * 128:(kc + 1) * 128], ("xg",),
                            ID_B, "id_bf")
                o, i = xTv[:, kc, j * 128:(j + 1) * 128], ptb[:, 0:128]
                P.op("vector", lambda e, o=o, i=i: e.tensor_copy(o, i),
                     reads=("ptb",), writes=("xT",))
        for fc in range(4):
            items = [(w1v[:, kc, fc * 128:(fc + 1) * 128], xTv[:, kc, :],
                      pav[:, fc, 0:384], ("w1b", "xT")) for kc in range(8)]
            P.matmuls("pa", items, each=False,
                      start=True, stop=True)
        tb = hT[:, :]
        sqr = gsc[:, :]
        hsrc = pav[:, :, 0:384]
        P.op("vector", lambda e, o=tb, i=hsrc: e.tensor_copy(
            o, i), reads=("pa",), writes=("hT",))
        P.op("scalar", lambda e, o=sqr, i=tb: e.activation(
            o, i, AF.Square), reads=("hT",), writes=("gsc",))
        P.op("vector", lambda e, o=sqr, b=tb: e.tensor_tensor(
            o, o, b, op=ALU.mult), reads=("gsc", "hT"), writes=("gsc",))
        P.op("vector", lambda e, o=sqr, b=tb: e.scalar_tensor_tensor(
            o, o, 0.044715, b, op0=ALU.mult, op1=ALU.add),
            reads=("gsc", "hT"), writes=("gsc",))
        P.op("scalar", lambda e, o=sqr: e.activation(
            o, o, AF.Tanh, bias=0.0, scale=0.7978845608028654),
            reads=("gsc",), writes=("gsc",))
        P.op("vector", lambda e, o=sqr: e.tensor_scalar(
            o, o, 1.0, None, op0=ALU.add), reads=("gsc",), writes=("gsc",))
        P.op("vector", lambda e, o=tb, b=sqr: e.tensor_tensor(
            o, o, b, op=ALU.mult), reads=("hT", "gsc"), writes=("hT",))
        P.op("vector", lambda e, o=tb: e.tensor_scalar(
            o, o, 0.5, None, op0=ALU.mult), reads=("hT",), writes=("hT",))
        for oc in range(8):
            items = [(w2v[:, fc, oc * 128:(oc + 1) * 128], hTv[:, fc, :],
                      pav[:, 0, :] if False else pa[:, 0:384],
                      ("w2b", "hT")) for fc in range(4)]
            P.matmuls("pa", items)
            o, i = yTv[:, oc, :], pa[:, 0:384]
            P.op("vector", lambda e, o=o, i=i: e.tensor_copy(o, i),
                 reads=("pa",), writes=("yT",))
        for j in range(3):
            for oc in range(8):
                P.transpose(ptb[:, 0:128], "ptb",
                            yTv[:, oc, j * 128:(j + 1) * 128], ("yT",),
                            ID_B, "id_bf")
                o, i = ywv[:, j, oc * 128:(oc + 1) * 128], ptb[:, 0:128]
                P.op("vector", lambda e, o=o, i=i: e.tensor_copy(o, i),
                     reads=("ptb",), writes=("ywst",))
            dst = yw_o[le * CAP1 + j * 128: le * CAP1 + (j + 1) * 128, :]
            P.dma("sync", dst, ywv[:, j, :], reads=("ywst",),
                  sem=f"yo{j % 3}")

    P.finish()
    return P.nc


def build_k9():
    P = Prog("k9")
    par = P.dram("ptok", [S, D], BF16, kind="ExternalInput").ap()
    ywi = P.dram("YW1", [NSLOT1, D], BF16, kind="ExternalInput").ap()
    slot_i = P.dram("slot6", [128, K1_ * NT], F32, kind="ExternalInput").ap()
    pw_i = P.dram("pw6", [128, K1_ * NT], F32, kind="ExternalInput").ap()
    out_o = P.dram("out", [S, D], F32, kind="ExternalOutput").ap()

    sl = P.sbuf("sl", [128, K1_ * NT], F32)
    pw = P.sbuf("pw", [128, K1_ * NT], F32)
    sli = P.sbuf("sli", [128, K1_ * NT], I32)
    xstg = P.sbuf("xstg", [128, 2 * D], BF16)
    gat = P.sbuf("gat", [128, 2 * D], BF16)
    acc = P.sbuf("acc", [128, D], F32)
    ob = P.sbuf("ob", [128, 2 * D], F32)

    P.dma("sync", sl[:, :], slot_i, writes=("sl",), sem="d0")
    P.dma("sync", pw[:, :], pw_i, writes=("pw",), sem="d0")
    o = sl[:, :]
    P.op("vector", lambda e, o=o: e.tensor_scalar(
        o, o, float(NSLOT1 - 1), None, op0=ALU.min), reads=("sl",),
        writes=("sl",))
    o2, i2 = sli[:, :], sl[:, :]
    P.op("vector", lambda e, o=o2, i=i2: e.tensor_copy(o, i), reads=("sl",),
         writes=("sli",))

    sliv = sli[:, :].rearrange("p (k j) -> p k j", k=K1_)
    pwv = pw[:, :].rearrange("p (k j) -> p k j", k=K1_)
    prv = par.rearrange("(j p) d -> j p d", p=128)
    outv = out_o.rearrange("(j p) d -> j p d", p=128)

    for j in range(NT):
        xsl = j % 2
        xdst = xstg[:, xsl * D:(xsl + 1) * D]
        P.dma("sync", xdst, prv[j], writes=(f"xst{xsl}",), sem=f"dx{xsl}")
        o = acc[:, :]
        P.op("vector", lambda e, o=o, i=xdst: e.tensor_copy(o, i),
             reads=(f"xst{xsl}",), writes=("acc",))
        for k in range(K1_):
            slot_col = sliv[:, k, j:j + 1]
            gdst = gat[:, (k % 2) * D:(k % 2 + 1) * D]
            gk = f"gt{k % 2}"
            P.idma(gdst, None, ywi, IndirectOffsetOnAxis(slot_col, axis=0),
                   reads=("sli",), writes=(gk,), sem=f"dg{k % 2}")
            wcol = pwv[:, k, j:j + 1]
            P.op("vector", lambda e, o=o, a=gdst, w=wcol:
                 e.scalar_tensor_tensor(o, a, w, o, op0=ALU.mult,
                                        op1=ALU.add),
                 reads=(gk, "pw", "acc"), writes=("acc",))
        os_ = ob[:, xsl * D:(xsl + 1) * D]
        P.op("vector", lambda e, o=os_, i=o: e.tensor_copy(o, i),
             reads=("acc",), writes=(f"ob{xsl}",))
        P.dma("sync", outv[j], os_, reads=(f"ob{xsl}",), sem=f"do{xsl}")

    P.finish()
    return P.nc


_NCACHE = {}


def _get_nc(name):
    if name not in _NCACHE:
        _NCACHE[name] = globals()["build_" + name]()
    return _NCACHE[name]


def _run(name, in_maps):
    nc = _get_nc(name)
    res = run_bass_kernel_spmd(nc, in_maps, core_ids=list(range(NCORES)))
    return res.results


def _t(a):
    return np.ascontiguousarray(a.T)


def kernel(**ins):
    x0 = ins["L0_attn_hidden_states"]
    wqT = _t(ins["L0_attn_Wq"])
    wkT = _t(ins["L0_attn_Wk"])
    wvT = _t(ins["L0_attn_Wv"])
    bq = ins["L0_attn_bq"].reshape(1, -1)
    bk = ins["L0_attn_bk"].reshape(1, -1)
    bv = ins["L0_attn_bv"].reshape(1, -1)
    wrT = _t(ins["L0_ffn_w_router"])
    wgT = _t(ins["L0_ffn_w_gate"])
    wd = ins["L0_ffn_w_down"]
    wqkv = ins["L1_attn_w_qkv"]
    bqkv = ins["L1_attn_b_qkv"].reshape(1, -1)
    wproj = ins["L1_attn_w_proj"]
    bproj = ins["L1_attn_b_proj"].reshape(1, -1)
    w64T = _t(ins["L1_ffn_router_weight"])
    b64 = ins["L1_ffn_expert_bias"].reshape(1, -1)
    w1 = ins["L1_ffn_w1"]
    w2 = ins["L1_ffn_w2"]
    sw1T = _t(ins["L1_ffn_sw1"])
    sw2T = _t(ins["L1_ffn_sw2"])

    x0T = [_t(x0[c]) for c in range(NCORES)]

    r1 = _run("k1", [{"x0T": x0T[c], "wqT": wqT, "wkT": wkT, "wvT": wvT,
                      "bq": bq, "bk": bk, "bv": bv} for c in range(NCORES)])
    r2 = _run("k2", [{"x0T": x0T[c], "ctxT": r1[c]["ctxT"],
                      "wrT": wrT, "wgT": wgT} for c in range(NCORES)])
    cnts = np.stack([r2[c]["cnt"][:, 0] for c in range(NCORES)])
    r3 = _run("k3", [{"idx2": r2[c]["idx2"], "gat2": r2[c]["gat2"],
                      "pos2": r2[c]["pos2"], "cnts": cnts}
                     for c in range(NCORES)])
    recs = np.stack([r3[c]["rec"] for c in range(NCORES)])
    htok = np.ascontiguousarray(
        np.concatenate([r2[c]["hT"] for c in range(NCORES)], axis=1).T)
    r4 = _run("k4", [{"recs": np.ascontiguousarray(
                          recs[:, c * CAP0:(c + 1) * CAP0, :]),
                      "htok": htok, "wdTe": _t(wd[c])}
                     for c in range(NCORES)])
    YW = np.concatenate([r4[c]["yw"] for c in range(NCORES)], axis=0)
    r5 = _run("k5", [{"x1tok": _t(r2[c]["x1T"]), "YW": YW,
                      "slot2": r3[c]["slot2"], "gn2": r3[c]["gn2"]}
                     for c in range(NCORES)])
    r6a = _run("k6a", [{"x2T": _t(r5[c]["x2tok"]), "w_qkv": wqkv,
                        "b_qkv": bqkv, "w_proj": wproj, "b_proj": bproj}
                       for c in range(NCORES)])
    r6b = _run("k6b", [{"x3T": r6a[c]["x3T"], "w64T": w64T, "b64": b64,
                        "sw1T": sw1T, "sw2T": sw2T} for c in range(NCORES)])
    cnts64 = np.stack([r6b[c]["cnt64"][:, 0] for c in range(NCORES)])
    r7 = _run("k7", [{"idx6": r6b[c]["idx6"], "pr6": r6b[c]["pr6"],
                      "pos6": r6b[c]["pos6"], "cnts64": cnts64}
                     for c in range(NCORES)])
    recs1 = np.stack([r7[c]["rec1"] for c in range(NCORES)])
    xtok = np.ascontiguousarray(
        np.concatenate([r6b[c]["xln4T"] for c in range(NCORES)], axis=1).T)
    r8 = _run("k8", [{"recs": np.ascontiguousarray(
                          recs1[:, c * 8 * CAP1:(c + 1) * 8 * CAP1, :]),
                      "xtok": xtok, "w1s": w1[c * 8:(c + 1) * 8],
                      "w2s": w2[c * 8:(c + 1) * 8]} for c in range(NCORES)])
    YW1 = np.concatenate([r8[c]["yw1"] for c in range(NCORES)], axis=0)
    r9 = _run("k9", [{"ptok": _t(r6b[c]["partialT"]), "YW1": YW1,
                      "slot6": r7[c]["slot6"], "pw6": r7[c]["pw6"]}
                     for c in range(NCORES)])
    return np.stack([r9[c]["out"] for c in range(NCORES)])

